# version 36
# baseline (speedup 1.0000x reference)
from contextlib import ExitStack

import numpy as np
import ml_dtypes
import concourse.bass as bass
import concourse.mybir as mybir
from concourse.bass_utils import run_bass_kernel_spmd

F32 = mybir.dt.float32
BF16 = mybir.dt.bfloat16
AF = mybir.ActivationFunctionType
ALU = mybir.AluOpType
AX = mybir.AxisListType

B, N, D = 8, 4096, 1024
H, HD = 16, 64
NT = N // 128
NE, M = 16, 512
NQB = 8
NPAIR = 8
NG = 4
GT = 8
EPS = 1e-5
INV_D = 1.0 / 1024.0

DEV_LIMIT = {"qb": NQB, "grp": NG, "nt": NT, "ne": NE + 1, "pre": True,
             "ln2": True}


class Sem:
    def __init__(self, nc, name):
        self.h = nc.alloc_semaphore(name)
        self.n = 0


class Dq:

    def __init__(self):
        self.items = []

    def wait(self, sem, val=None):
        v = sem.n if val is None else val
        if v > 0:
            self.items.append(lambda e, h=sem.h, v=v: e.wait_ge(h, v))

    def run(self, fn, inc=None, k=1):
        if inc is not None:
            inc.n += k
            self.items.append(
                lambda e, fn=fn, h=inc.h, k=k: fn(e).then_inc(h, k))
            return inc.n
        self.items.append(fn)
        return None

    def flush(self, section):
        items, self.items = self.items, []
        if not items:
            return

        @section
        def _(eng):
            for it in items:
                it(eng)


def build_program(dev_dump=()):
    nc = bass.Bass("TRN2", target_bir_lowering=False)

    x_in = nc.dram_tensor("x", [N, D], F32, kind="ExternalInput").ap()
    w_qkvT = nc.dram_tensor("w_qkvT", [D, 3 * D], F32, kind="ExternalInput").ap()
    w_projT = nc.dram_tensor("w_projT", [D, D], F32, kind="ExternalInput").ap()
    b_proj = nc.dram_tensor("b_proj", [1, D], F32, kind="ExternalInput").ap()
    gate_pk = nc.dram_tensor("gate_pk", [128, 128], F32, kind="ExternalInput").ap()
    w_gate = nc.dram_tensor("w_gate", [NE, D, M], F32, kind="ExternalInput").ap()
    w_up = nc.dram_tensor("w_up", [NE, D, M], F32, kind="ExternalInput").ap()
    w_down = nc.dram_tensor("w_down", [NE, M, D], F32, kind="ExternalInput").ap()
    ws_gate = nc.dram_tensor("ws_gate", [D, M], F32, kind="ExternalInput").ap()
    ws_up = nc.dram_tensor("ws_up", [D, M], F32, kind="ExternalInput").ap()
    ws_down = nc.dram_tensor("ws_down", [M, D], F32, kind="ExternalInput").ap()
    cos_pk = nc.dram_tensor("cos_pk", [128, NT * 32], F32, kind="ExternalInput").ap()
    sin_pk = nc.dram_tensor("sin_pk", [128, NT * 32], F32, kind="ExternalInput").ap()
    eye_in = nc.dram_tensor("eye_in", [128, 128], BF16, kind="ExternalInput").ap()
    eye32_in = nc.dram_tensor("eye32_in", [128, 128], F32,
                              kind="ExternalInput").ap()

    def scratch(name, shape, dtype):
        kind = "ExternalOutput" if name in dev_dump else "Internal"
        return nc.dram_tensor(name, shape, dtype, kind=kind).ap()

    q_scr = scratch("q_scr", [NT, 128, 1024], BF16)
    k_scr = scratch("k_scr", [NPAIR, 128, N], BF16)
    v_scr = scratch("v_scr", [NPAIR, 128, N], BF16)
    s1_scr = scratch("s1_scr", [NT, 128, D], F32)
    x1_scr = scratch("x1_scr", [NT, 128, D], F32)
    out = nc.dram_tensor("out", [N, D], F32, kind="ExternalOutput").ap()

    sd = Sem(nc, "sd")
    dw = Sem(nc, "dw")
    pe = Sem(nc, "pe")
    ve = Sem(nc, "ve")
    act = Sem(nc, "act")
    gp = Sem(nc, "gp")
    m = {}

    ones_sb = nc.alloc_sbuf_tensor("ones_sb", [128, 128], BF16)
    eye_sb = nc.alloc_sbuf_tensor("eye_sb", [128, 128], BF16)

    with ExitStack() as _st:
        _sb = lambda n_, s_, d_: _st.enter_context(nc.sbuf_tensor(n_, s_, d_))
        _ps = lambda n_, s_, d_: _st.enter_context(nc.psum_tensor(n_, s_, d_))
        w_all = _sb("w_all", [128, 8 * 3072], BF16)
        cos_sb = _sb("cos_sb", [128, NT * 32], F32)
        sin_sb = _sb("sin_sb", [128, NT * 32], F32)
        x_bf = [_sb(f"x_bf{i}", [128, D], BF16) for i in range(4)]
        x_fm = [_sb(f"x_fm{i}", [128, D], BF16) for i in range(4)]
        rt = [_sb(f"rt{i}", [128, 256], F32) for i in range(16)]
        roped = [_sb(f"roped{i}", [128, 512], BF16) for i in range(4)]
        q_t = [_sb(f"q_t{i}", [128, 1024], BF16) for i in range(3)]
        k_t = [_sb(f"k_t{i}", [128, 1024], BF16) for i in range(3)]
        v_t = [_sb(f"v_t{i}", [128, 1024], BF16) for i in range(3)]
        ps_x = [_ps(f"ps_xr{i}", [128, 1024], BF16) for i in range(2)]
        ps_qk = [_ps(f"ps_qk{i}", [128, 512], F32) for i in range(3)]
        ps_t = [_ps(f"ps_t{i}", [128, 512], BF16) for i in range(3)]
        blk = _st.enter_context(nc.Block(no_gpsimd_drain=True))
        qSY, qPE, qV, qA, qG = Dq(), Dq(), Dq(), Dq(), Dq()

        qSY.run(lambda e: e.dma_start(cos_sb[:], cos_pk), inc=sd, k=16)
        qSY.run(lambda e: e.dma_start(sin_sb[:], sin_pk), inc=sd, k=16)
        qSY.run(lambda e: e.dma_start(eye_sb[:], eye_in), inc=sd, k=16)
        m["const"] = sd.n

        for ec in range(6):
            def _ldw(e, ec=ec):
                src = w_qkvT[:, ec * 512:(ec + 1) * 512].rearrange(
                    "(c p) e2 -> p c e2", p=128)
                dst = w_all[:].rearrange("p (c e2) -> p c e2", e2=3072)[
                    :, :, ec * 512:(ec + 1) * 512]
                return e.dma_start(dst, src)
            qG.run(_ldw, inc=dw, k=16)
        m["w_qkv"] = dw.n

        qV.wait(sd, m["const"])
        qV.run(lambda e: e.memset(ones_sb[:], 1.0), inc=ve)
        m["ones"] = ve.n

        pend_tev = {}
        pend_rT = []
        pend_comb = []
        vei = [0]
        rope_vei = {}

        def try_flush_combs():
            while pend_comb and vei[0] - rope_vei[pend_comb[0]] >= 10:
                emit_comb(pend_comb.pop(0))

        def emit_comb(qkc, on_gp=False):
            rb = (qkc % 4) * 4
            r32 = lambda tt: tt[:].rearrange("p (h f) -> p h f", f=32)

            def _comb(e, qkc=qkc, rb=rb, r32=r32):
                rv = roped[qkc % 4][:].rearrange("p (h f) -> p h f", f=64)
                e.tensor_tensor(rv[:, :, 0:32], r32(rt[rb]),
                                r32(rt[rb + 1]), ALU.subtract)
                return e.tensor_tensor(rv[:, :, 32:64], r32(rt[rb + 2]),
                                       r32(rt[rb + 3]), ALU.add)
            if on_gp:
                qG.wait(ve, m["rope_q", qkc])
                if ("ropeTq", qkc - 4) in m:
                    qG.wait(pe, m["ropeTq", qkc - 4])
                qG.run(_comb, inc=gp)
                m["comb", qkc] = ("gp", gp.n)
            else:
                if ("ropeTq", qkc - 4) in m:
                    qV.wait(pe, m["ropeTq", qkc - 4])
                qV.run(_comb, inc=ve)
                vei[0] += 2
                m["comb", qkc] = ("ve", ve.n)

        def emit_ropeT(qkc):
            ceng, cnt = m["comb", qkc]
            qPE.wait(ve if ceng == "ve" else gp, cnt)
            if ("tevq", qkc - 3) in m:
                qPE.wait(ve, m["tevq", qkc - 3])
            for j in range(4):
                qPE.run(lambda e, qkc=qkc, j=j: e.transpose(
                    ps_t[qkc % 3][:, j * 128:(j + 1) * 128],
                    roped[qkc % 4][:, j * 128:(j + 1) * 128],
                    eye_sb[:]),
                    inc=pe if j == 3 else None)
            m["ropeTq", qkc] = pe.n

        def emit_tev(qkc):
            b3_, c_, gc_ = pend_tev.pop(qkc)
            qV.wait(pe, m["ropeTq", qkc])
            T_ = qkc // 4
            if T_ >= 3:
                qV.wait(sd, m["scrout", T_ - 3])

            def _tev(e, b3_=b3_, c_=c_, qkc=qkc):
                if c_ < 2:
                    dst = q_t[b3_][:, c_ * 512:(c_ + 1) * 512]
                else:
                    dst = k_t[b3_][:, (c_ - 2) * 512:(c_ - 1) * 512]
                return e.tensor_copy(dst, ps_t[qkc % 3][:])
            qV.run(_tev, inc=ve)
            vei[0] += 1
            m["tevq", qkc] = ve.n
            m["tev", gc_] = ve.n
            try_flush_combs()

        for T in range(DEV_LIMIT["nt"]):
            bi = T % 4
            b3 = T % 3
            if T >= 4:
                qG.wait(pe, m["xT", T - 4])
            qG.run(lambda e, T=T, bi=bi: e.dma_start(
                x_bf[bi][:], x_in[T * 128:(T + 1) * 128, :]), inc=dw, k=16)
            m["xcast", T] = dw.n

            qPE.wait(dw, m["xcast", T])
            if T >= 2:
                qPE.wait(ve, m["xfm_evac", T - 2])
            for c2 in range(8):
                qPE.run(lambda e, bi=bi, c2=c2, T=T: e.transpose(
                    ps_x[T % 2][:, c2 * 128:(c2 + 1) * 128],
                    x_bf[bi][:, c2 * 128:(c2 + 1) * 128], eye_sb[:]),
                    inc=pe if c2 == 7 else None)
            m["xT", T] = pe.n

            qV.wait(pe, m["xT", T])
            if T >= 4:
                qV.wait(pe, m["mm", T - 4, 5])
            qV.run(lambda e, bi=bi, T=T: e.tensor_copy(
                x_fm[bi][:], ps_x[T % 2][:]), inc=ve)
            vei[0] += 1
            m["xfm_evac", T] = ve.n
            try_flush_combs()

            for c in range(6):
                gc = T * 6 + c
                pq = gc % 3
                qPE.wait(ve, m["xfm_evac", T])
                if T == 0 and c == 0:
                    qPE.wait(dw, m["w_qkv"])
                if ("chunk_done", gc - 3) in m:
                    qPE.wait(ve, m["chunk_done", gc - 3])
                for hc in range(8):
                    qPE.run(lambda e, bi=bi, c=c, hc=hc, pq=pq: e.matmul(
                        ps_qk[pq][:],
                        x_fm[bi][:, hc * 128:(hc + 1) * 128],
                        w_all[:, hc * 3072 + c * 512: hc * 3072 + (c + 1) * 512],
                        start=(hc == 0), stop=(hc == 7)),
                        inc=pe if hc == 7 else None)
                m["mm", T, c] = pe.n

                r32 = lambda tt: tt[:].rearrange("p (h f) -> p h f", f=32)
                if c < 4:
                    qkc = T * 4 + c
                    rb = (qkc % 4) * 4
                    qV.wait(pe, m["mm", T, c])
                    if ("comb", qkc - 4) in m:
                        ce_, cn_ = m["comb", qkc - 4]
                        qV.wait(ve if ce_ == "ve" else gp, cn_)

                    def _rope(e, T=T, pq=pq, rb=rb):
                        p8 = ps_qk[pq][:].rearrange("p (h f) -> p h f", f=64)
                        x1h, x2h = p8[:, :, 0:32], p8[:, :, 32:64]
                        cb = cos_sb[:, T * 32:(T + 1) * 32].unsqueeze(1
                                    ).broadcast_to([128, 8, 32])
                        sn = sin_sb[:, T * 32:(T + 1) * 32].unsqueeze(1
                                    ).broadcast_to([128, 8, 32])
                        e.tensor_tensor(r32(rt[rb]), x1h, cb, ALU.mult)
                        e.tensor_tensor(r32(rt[rb + 1]), x2h, sn, ALU.mult)
                        e.tensor_tensor(r32(rt[rb + 2]), x2h, cb, ALU.mult)
                        return e.tensor_tensor(r32(rt[rb + 3]), x1h, sn, ALU.mult)
                    qV.run(_rope, inc=ve)
                    vei[0] += 4
                    m["rope", gc] = ve.n
                    m["rope_q", qkc] = ve.n
                    rope_vei[qkc] = vei[0]
                    m["chunk_done", gc] = ve.n

                    pend_comb.append(qkc)
                    try_flush_combs()

                    pend_rT.append(qkc)
                    pend_tev[qkc] = (b3, c, gc)
                else:
                    qV.wait(pe, m["mm", T, c])
                    if T >= 3:
                        qV.wait(sd, m["scrout", T - 3])
                    qV.run(lambda e, b3=b3, c=c, pq=pq: e.tensor_copy(
                        v_t[b3][:, (c - 4) * 512:(c - 3) * 512], ps_qk[pq][:]),
                        inc=ve)
                    vei[0] += 1
                    m["tev", gc] = ve.n
                    m["chunk_done", gc] = ve.n
                    try_flush_combs()

                if pend_rT and (len(pend_rT) >= 3 or c >= 4):
                    qk_f = pend_rT.pop(0)
                    while pend_comb and pend_comb[0] <= qk_f:
                        emit_comb(pend_comb.pop(0), on_gp=True)
                    emit_ropeT(qk_f)
                    if qk_f - 1 in pend_tev:
                        emit_tev(qk_f - 1)

            while pend_rT:
                qk_f = pend_rT.pop(0)
                while pend_comb and pend_comb[0] <= qk_f:
                    emit_comb(pend_comb.pop(0), on_gp=True)
                emit_ropeT(qk_f)
                if qk_f - 1 in pend_tev:
                    emit_tev(qk_f - 1)
            for qkc_ in (T * 4 + 2, T * 4 + 3):
                if qkc_ in pend_tev:
                    emit_tev(qkc_)
            qSY.wait(ve, m["tev", T * 6 + 5])
            qSY.wait(ve, m["tevq", T * 4 + 3])
            qSY.run(lambda e, T=T, b3=b3: e.dma_start(
                q_scr[T, :, :], q_t[b3][:]), inc=sd, k=16)

            def _kout(e, T=T, b3=b3):
                ksrc = k_t[b3][:].rearrange("q (r f) -> q r f", f=128)
                kdst = k_scr[:, :, T * 128:(T + 1) * 128].transpose([1, 0, 2])
                return e.dma_start(kdst, ksrc)
            qSY.run(_kout, inc=sd, k=16)

            def _vout(e, T=T, b3=b3):
                vsrc = v_t[b3][:].rearrange("q (r f) -> q r f", f=128)
                vdst = v_scr[:, :, T * 128:(T + 1) * 128].transpose([1, 0, 2])
                return e.dma_start(vdst, vsrc)
            qSY.run(_vout, inc=sd, k=16)
            m["scrout", T] = sd.n

        qSY.wait(sd)
        qSY.flush(blk.sync)
        qPE.flush(blk.tensor)
        qV.flush(blk.vector)
        qA.flush(blk.scalar)
        qG.flush(blk.gpsimd)

    with ExitStack() as _st:
        _sb = lambda n_, s_, d_: _st.enter_context(nc.sbuf_tensor(n_, s_, d_))
        _ps = lambda n_, s_, d_: _st.enter_context(nc.psum_tensor(n_, s_, d_))
        w_proj_sb = _sb("w_proj_sb", [128, 8 * 1024], BF16)
        kp = [_sb(f"kp{i}", [128, N], BF16) for i in range(3)]
        vp = [_sb(f"vp{i}", [128, N], BF16) for i in range(3)]
        pt = [_sb(f"pt{i}", [128, 8 * 1024], BF16) for i in range(5)]
        q_qb = [_sb(f"q_qb{i}", [128, 4096], BF16) for i in range(2)]
        o_all = [_sb(f"o_all{i}", [128, 8 * 512], BF16) for i in range(2)]
        rrow = _sb("rrow", [128, 512], BF16)
        r_sb = _sb("r_sb", [128, 512], BF16)
        xres = [_sb(f"xres{i}", [128, D], F32) for i in range(2)]
        ps_s = [_ps(f"ps_s{i}", [128, 1024], F32) for i in range(2)]
        ps_oA = _ps("ps_oA", [128, 512], F32)
        ps_oB = _ps("ps_oB", [128, 512], F32)
        ps_dA = _ps("ps_dA", [128, 512], F32)
        ps_dB = _ps("ps_dB", [128, 512], F32)
        qr = [q_qb[i][:].rearrange("p (t r f) -> p t r f", r=8, f=128)
              for i in range(2)]

        with nc.Block(no_gpsimd_drain=True) as blk:
            @blk.gpsimd
            def _(g):
                src = w_projT[:].rearrange("(c p) e2 -> p c e2", p=128)
                dst = w_proj_sb[:].rearrange("p (c e2) -> p c e2", e2=1024)
                g.dma_start(dst, src).then_inc(dw.h, 16)
                dw.n += 16
            m["w_proj"] = dw.n

        NJ = DEV_LIMIT["qb"] * NPAIR
        with nc.Block(no_gpsimd_drain=True) as blk:
            qSY, qPE, qV, qA = Dq(), Dq(), Dq(), Dq()

            def emit_qload(qb):
                if qb >= DEV_LIMIT["qb"]:
                    return
                if qb >= 2:
                    qPE_mark = m["Sdone", (qb - 2) * 8 + 7]
                    qSY.wait(pe, qPE_mark)
                for i in range(4):
                    qSY.run(lambda e, qb=qb, i=i: e.dma_start(
                        q_qb[qb % 2][:, i * 1024:(i + 1) * 1024],
                        q_scr[qb * 4 + i, :, :]), inc=sd, k=16)
                m["qld", qb] = sd.n

            def emit_prefetch(j):
                if j + 2 >= NJ:
                    return
                qSY.wait(pe, m["Sdone", j - 1])
                qSY.wait(pe, m["odone", j - 1])
                qSY.run(lambda e, j=j: e.dma_start(
                    kp[(j + 2) % 3][:], k_scr[(j + 2) % 8, :, :]), inc=sd, k=16)
                qSY.run(lambda e, j=j: e.dma_start(
                    vp[(j + 2) % 3][:], v_scr[(j + 2) % 8, :, :]), inc=sd, k=16)
                m["kv", j + 2] = sd.n

            def emit_s_kt(j, s4, kt, act_pb):
                qb = j // 8
                if s4 == 0 and kt == 0:
                    qPE.wait(sd, m["kv", j])
                    qPE.wait(sd, m["qld", qb])
                gkt = s4 * 4 + kt
                if gkt >= 2:
                    tg = gkt - 2
                    qPE.wait(act, act_pb + tg + 1 +
                             (1 if (tg >= 28 and j >= 1) else 0))
                elif j >= 1:
                    qPE.wait(act, m["exp", j - 1, 30 + gkt])

                def _smm(e, j=j, qb=qb, gkt=gkt):
                    qrv = qr[qb % 2]
                    for hh, base in ((0, 0), (1, 64)):
                        inst = e.matmul(
                            ps_s[gkt % 2][:, hh * 512:(hh + 1) * 512],
                            kp[j % 3][base:base + 64,
                                      gkt * 128:(gkt + 1) * 128],
                            qrv[base:base + 64, :, j % 8, :],
                            start=True, stop=True)
                    return inst
                qPE.run(_smm, inc=pe)
                m["S", j, gkt] = pe.n
                if gkt == 31:
                    m["Sdone", j] = pe.n

            def emit_exp_slice(j, s4, act_pb):
                gi = j * 4 + s4 // 2
                buf = gi % 5
                if s4 % 2 == 0 and gi >= 5:
                    qA.wait(pe, m["od", gi - 5])
                for kt in range(4):
                    gkt = s4 * 4 + kt
                    qA.wait(pe, m["S", j, gkt])
                    lkt = gkt - (s4 // 2) * 8
                    qA.run(lambda e, buf=buf, lkt=lkt, gkt=gkt: e.activation(
                        pt[buf][:, lkt * 1024:(lkt + 1) * 1024],
                        ps_s[gkt % 2][:], AF.Exp, scale=0.125), inc=act)
                    assert act.n == act_pb + gkt + 1 + (
                        1 if (gkt >= 28 and j >= 1) else 0)
                    m["exp", j, gkt] = act.n

            def emit_o_kt(j, s4, kt):
                if s4 == 0 and kt == 0 and j >= 1:
                    qPE.wait(ve, m["omul", j - 1])
                    if j % 8 == 0 and j >= 8:
                        qPE.wait(ve, m["projev", (j // 8 - 1) * 4 + 3])
                gi = j * 4 + s4 // 2
                buf = gi % 5
                gkt = s4 * 4 + kt
                qPE.wait(act, m["exp", j, gkt])
                lkt = gkt - (s4 // 2) * 8

                def _omm(e, j=j, buf=buf, lkt=lkt, gkt=gkt):
                    rhsA = pt[buf][:, lkt * 1024: lkt * 1024 + 512]
                    rhsB = pt[buf][:, lkt * 1024 + 512: (lkt + 1) * 1024]
                    e.matmul(ps_oA[0:64, :],
                             vp[j % 3][:, gkt * 128: gkt * 128 + 64],
                             rhsA, start=(gkt == 0), stop=(gkt == 31))
                    return e.matmul(
                        ps_oB[64:128, :],
                        vp[j % 3][:, gkt * 128 + 64: (gkt + 1) * 128],
                        rhsB, start=(gkt == 0), stop=(gkt == 31),
                        tile_position=(0, 64))
                qPE.run(_omm, inc=pe if kt == 3 else None)
                if kt == 3 and s4 % 2 == 1:
                    m["od", gi] = pe.n
                if kt == 3 and s4 == 7:
                    m["odone", j] = pe.n

            def emit_d_kt(j, s4, kt):
                if s4 == 0 and kt == 0 and j >= 1:
                    qPE.wait(act, m["rev", j - 1])
                gi = j * 4 + s4 // 2
                buf = gi % 5
                gkt = s4 * 4 + kt
                qPE.wait(act, m["exp", j, gkt])
                lkt = gkt - (s4 // 2) * 8

                def _dmm(e, buf=buf, lkt=lkt, gkt=gkt):
                    rhsA = pt[buf][:, lkt * 1024: lkt * 1024 + 512]
                    rhsB = pt[buf][:, lkt * 1024 + 512: (lkt + 1) * 1024]
                    e.matmul(ps_dA[0:1, :], ones_sb[:, 0:1],
                             rhsA, start=(gkt == 0), stop=(gkt == 31))
                    return e.matmul(
                        ps_dB[32:33, :], ones_sb[:, 0:1],
                        rhsB, start=(gkt == 0), stop=(gkt == 31),
                        tile_position=(0, 32))
                qPE.run(_dmm, inc=pe if (kt == 3 and s4 == 7) else None)
                if kt == 3 and s4 == 7:
                    m["ddone", j] = pe.n

            def emit_norm_head(j):
                qV.wait(pe, m["ddone", j])

                def _recip(e):
                    with nc.allow_low_precision(
                            reason="softmax 1/denom feeds bf16 bcast mm"):
                        e.reciprocal(rrow[0:1, :], ps_dA[0:1, :])
                        return e.reciprocal(rrow[32:33, :], ps_dB[32:33, :])
                qV.run(_recip, inc=ve)
                m["recip", j] = ve.n

                qPE.wait(ve, m["recip", j])

                def _bc(e):
                    e.matmul(ps_dA[0:64, :], ones_sb[0:1, 0:64],
                             rrow[0:1, :], start=True, stop=True)
                    return e.matmul(ps_dB[64:128, :], ones_sb[32:33, 0:64],
                                    rrow[32:33, :], start=True, stop=True,
                                    tile_position=(32, 64))
                qPE.run(_bc, inc=pe)
                m["bc", j] = pe.n

                qA.wait(pe, m["bc", j])

                def _rev(e):
                    e.copy(r_sb[0:64, :], ps_dA[0:64, :])
                    return e.copy(r_sb[64:128, :], ps_dB[64:128, :])
                qA.run(_rev, inc=act)
                m["rev", j] = act.n

            def emit_norm_tail(j):
                qV.wait(act, m["rev", j])
                qV.wait(pe, m["odone", j])
                p8 = j % 8
                qb = j // 8
                if qb >= 2 and ("proj", (qb - 2) * 4 + 3) in m:
                    qV.wait(pe, m["proj", (qb - 2) * 4 + 3])

                def _omul(e, p8=p8, qb=qb):
                    oa = o_all[qb % 2]
                    e.tensor_tensor(oa[0:64, p8 * 512:(p8 + 1) * 512],
                                    ps_oA[0:64, :], r_sb[0:64, :], ALU.mult)
                    return e.tensor_tensor(
                        oa[64:128, p8 * 512:(p8 + 1) * 512],
                        ps_oB[64:128, :], r_sb[64:128, :], ALU.mult)
                qV.run(_omul, inc=ve)
                m["omul", j] = ve.n

            def emit_proj(qb):
                for s in range(4):
                    gsub = qb * 4 + s
                    if gsub >= 2:
                        qSY.wait(sd, m["s1w", gsub - 2])
                    qSY.run(lambda e, qb=qb, s=s: e.dma_start(
                        xres[s % 2][:],
                        x_in[qb * 512 + s * 128: qb * 512 + (s + 1) * 128, :]),
                        inc=sd, k=16)
                    m["xres", gsub] = sd.n

                    qPE.wait(ve, m["omul", qb * 8 + 7])
                    if qb == 0 and s == 0:
                        qPE.wait(dw, m["w_proj"])
                    if gsub >= 1:
                        qPE.wait(ve, m["projev", gsub - 1])
                    for nh, pso in ((0, ps_oA), (1, ps_oB)):
                        for pp in range(8):
                            qPE.run(lambda e, qb=qb, s=s, nh=nh, pp=pp, pso=pso:
                                    e.matmul(
                                pso[:],
                                o_all[qb % 2][:, pp * 512 + s * 128: pp * 512 + (s + 1) * 128],
                                w_proj_sb[:, pp * 1024 + nh * 512: pp * 1024 + (nh + 1) * 512],
                                start=(pp == 0), stop=(pp == 7)),
                                inc=pe if (nh == 1 and pp == 7) else None)
                    m["proj", gsub] = pe.n

                    qV.wait(pe, m["proj", gsub])
                    qV.wait(sd, m["xres", gsub])

                    def _prev(e, s=s):
                        e.tensor_tensor(xres[s % 2][:, 0:512], ps_oA[:],
                                        xres[s % 2][:, 0:512], ALU.add)
                        return e.tensor_tensor(xres[s % 2][:, 512:1024], ps_oB[:],
                                               xres[s % 2][:, 512:1024], ALU.add)
                    qV.run(_prev, inc=ve)
                    m["projev", gsub] = ve.n

                    qSY.wait(ve, m["projev", gsub])
                    qSY.run(lambda e, qb=qb, s=s: e.dma_start(
                        s1_scr[qb * 4 + s, :, :], xres[s % 2][:]), inc=sd, k=16)
                    m["s1w", gsub] = sd.n

            emit_qload(0)
            emit_qload(1)
            qSY.run(lambda e: e.dma_start(kp[0][:], k_scr[0, :, :]), inc=sd, k=16)
            qSY.run(lambda e: e.dma_start(vp[0][:], v_scr[0, :, :]), inc=sd, k=16)
            m["kv", 0] = sd.n
            qSY.run(lambda e: e.dma_start(kp[1][:], k_scr[1, :, :]), inc=sd, k=16)
            qSY.run(lambda e: e.dma_start(vp[1][:], v_scr[1, :, :]), inc=sd, k=16)
            m["kv", 1] = sd.n
            qSY.run(lambda e: e.dma_start(kp[2][:], k_scr[2, :, :]), inc=sd, k=16)
            qSY.run(lambda e: e.dma_start(vp[2][:], v_scr[2, :, :]), inc=sd, k=16)
            m["kv", 2] = sd.n

            dlead = {0: [0], 1: [1], 2: [2], 3: [3, 4], 4: [5],
                     5: [6], 6: [7], 7: []}
            for j in range(NJ):
                act_pb = act.n
                for s4 in range(8):
                    for kt in range(4):
                        emit_s_kt(j, s4, kt, act_pb)
                        if j >= 1:
                            for sd4 in dlead[s4]:
                                emit_d_kt(j - 1, sd4, kt)
                            emit_o_kt(j - 1, s4, kt)
                    emit_exp_slice(j, s4, act_pb)
                    if j >= 1 and s4 == 6:
                        emit_norm_head(j - 1)
                    if j >= 1 and s4 == 7:
                        emit_norm_tail(j - 1)
                        emit_prefetch(j)
                        if (j - 1) % 8 == 7:
                            emit_proj((j - 1) // 8)
                            emit_qload((j - 1) // 8 + 2)
            for s4 in range(8):
                for kt in range(4):
                    for sd4 in dlead[s4]:
                        emit_d_kt(NJ - 1, sd4, kt)
                    emit_o_kt(NJ - 1, s4, kt)
            emit_norm_head(NJ - 1)
            emit_norm_tail(NJ - 1)
            emit_proj(NJ // 8 - 1)

            qSY.wait(sd)
            qSY.flush(blk.sync)
            qPE.flush(blk.tensor)
            qV.flush(blk.vector)
            qA.flush(blk.scalar)


    with ExitStack() as _st:
        _sb = lambda n_, s_, d_: _st.enter_context(nc.sbuf_tensor(n_, s_, d_))
        _ps = lambda n_, s_, d_: _st.enter_context(nc.psum_tensor(n_, s_, d_))
        wg = [_sb(f"wg{i}", [128, 8 * 512], BF16) for i in range(2)]
        wu = [_sb(f"wu{i}", [128, 8 * 512], BF16) for i in range(2)]
        wd = [_sb(f"wd{i}", [128, 4 * 1024], BF16) for i in range(2)]
        x1fm = [_sb(f"x1fm{i}", [128, 8 * 1024], BF16) for i in range(2)]
        x1loT = _sb("x1loT", [128, 1024], BF16)
        acc_g = _sb("acc_g", [128, GT * 1024], F32)
        s1t = [_sb(f"s1t{i}", [128, D], F32) for i in range(4)]
        x1t = [_sb(f"x1t{i}", [128, D], F32) for i in range(2)]
        x1v = [_sb(f"x1v{i}", [128, D], F32) for i in range(4)]
        ot = [_sb(f"ot{i}", [128, D], F32) for i in range(2)]
        b_bc = _sb("b_bc", [128, D], F32)
        b_rbf = _sb("b_rbf", [1, D], BF16)
        gate_sb = _sb("gate_sb", [128, 128], F32)
        ghi = _sb("ghi", [128, 128], BF16)
        glo = _sb("glo", [128, 128], BF16)
        gs = [_sb(f"gs{i}", [128, 512], BF16) for i in range(2)]
        guT = [_sb(f"guT{i}", [128, 4 * 512], BF16) for i in range(2)]
        S_all = _sb("S_all", [128, 128], F32)
        Ssig = _sb("Ssig", [128, 128], F32)
        msk_sb = _sb("msk_sb", [128, 16], F32)
        msk2_sb = _sb("msk2_sb", [128, 16], F32)
        sg_sb = _sb("sg_sb", [128, 16], F32)
        ge2_sb = _sb("ge2_sb", [128, 16], F32)
        rt_sb = _sb("rt_sb", [128, 64], F32)
        route_g = [_sb(f"route{i}", [128, GT * 16], F32) for i in range(2)]
        stats = _sb("stats", [128, 128], F32)
        rstats = _sb("rstats", [128, 128], F32)
        eps_sb = _sb("eps_sb", [128, 1], F32)
        eye32_sb = _sb("eye32_sb", [128, 128], F32)
        ps_pre = _ps("ps_pre", [128, 512], F32)
        ps_g = [_ps(f"ps_g{i}", [128, 512], F32) for i in range(2)]
        ps_u = _ps("ps_u", [128, 512], F32)
        ps_dn = [_ps(f"ps_dn{i}", [128, 1024], F32) for i in range(2)]

        def col(k, ti):
            return stats[:, k * 8 + ti: k * 8 + ti + 1]

        def rcol(k, ti):
            return rstats[:, k * 8 + ti: k * 8 + ti + 1]

        def rtc(k, ti):
            return rt_sb[:, k * 8 + ti: k * 8 + ti + 1]

        with nc.Block(no_gpsimd_drain=True) as blk:
            qSY, qPE, qV, qA, qG = Dq(), Dq(), Dq(), Dq(), Dq()
            def _ldb(e):
                return e.dma_start(b_rbf[:], b_proj)
            qG.run(_ldb, inc=dw, k=16)
            m["b_rbf"] = dw.n
            qSY.run(lambda e: e.dma_start(gate_sb[:], gate_pk), inc=sd, k=16)
            qSY.run(lambda e: e.dma_start(eye32_sb[:], eye32_in), inc=sd, k=16)
            m["binit"] = sd.n

            qA.wait(sd, m["binit"])
            qA.run(lambda e: e.copy(ghi[:], gate_sb[:]), inc=act)
            m["bcast0"] = act.n

            qV.wait(act, m["bcast0"])
            qV.run(lambda e: e.memset(eps_sb[:], EPS))
            qV.run(lambda e: e.tensor_tensor(glo[:], gate_sb[:], ghi[:],
                                             ALU.subtract), inc=ve)
            m["glo"] = ve.n

            qPE.wait(dw, m["b_rbf"])
            qPE.wait(ve, m["ones"])
            qPE.run(lambda e: e.matmul(ps_dn[0][:, 0:512], ones_sb[0:1, :],
                                       b_rbf[0:1, 0:512], start=True, stop=True))
            qPE.run(lambda e: e.matmul(ps_dn[0][:, 512:1024], ones_sb[0:1, :],
                                       b_rbf[0:1, 512:1024], start=True,
                                       stop=True), inc=pe)
            m["bps"] = pe.n

            qA.wait(pe, m["bps"])
            qA.run(lambda e: e.copy(b_bc[:], ps_dn[0][:]), inc=act)
            m["bbc"] = act.n
            qSY.flush(blk.sync)
            qPE.flush(blk.tensor)
            qV.flush(blk.vector)
            qA.flush(blk.scalar)
            qG.flush(blk.gpsimd)

        def load_w(qG, g_, e_, buf):
            srcs = ([w_gate[e_], w_up[e_], w_down[e_]] if e_ < NE
                    else [ws_gate, ws_up, ws_down])
            for src, dstt, width in ((srcs[0], wg[buf], 512),
                                     (srcs[1], wu[buf], 512),
                                     (srcs[2], wd[buf], 1024)):
                def _ld(e, src=src, dstt=dstt, width=width):
                    sap = src.rearrange("(c p) m2 -> p c m2", p=128)
                    dap = dstt[:].rearrange("p (c m2) -> p c m2", m2=width)
                    return e.dma_start(dap, sap)
                qG.run(_ld, inc=dw, k=16)

        NGRP = DEV_LIMIT["grp"]
        NEX = DEV_LIMIT["ne"]
        silu_hist, gu_hist, dn_hist, dmm_hist = [], [], [], []
        with nc.Block(no_gpsimd_drain=True) as blk:
            qSY, qPE, qV, qA, qG = Dq(), Dq(), Dq(), Dq(), Dq()

            def emit_pre_tile(gn, ti):
                b4 = ti % 4

                qw = (m.get(("l1ap", gn, ti - 4)) if ti >= 4
                      else m.get(("l1ap", gn - 1, 4 + ti)))
                if qw is not None:
                    qSY.wait(act, qw)
                qSY.run(lambda e, gn=gn, ti=ti, b4=b4: e.dma_start(
                    s1t[b4][:], s1_scr[gn * 8 + ti, :, :]), inc=sd, k=16)
                m["s1ld", gn, ti] = sd.n

                qV.wait(sd, m["s1ld", gn, ti])
                if gn == 0 and ti == 0:
                    qV.wait(act, m["bbc"])
                qV.run(lambda e, b4=b4: e.tensor_tensor(
                    s1t[b4][:], s1t[b4][:], b_bc[:], ALU.add), inc=ve)
                m["v1", gn, ti] = ve.n

                qA.wait(ve, m["v1", gn, ti])
                bi = ti % 2
                qw = (m.get(("tp1", gn, ti - 2)) if ti >= 2
                      else m.get(("tp1", gn - 1, 6 + bi)))
                if qw is not None:
                    qA.wait(pe, qw)
                qw = (m.get(("x1w", gn, ti - 2)) if ti >= 2
                      else m.get(("x1w", gn - 1, 6 + bi)))
                if qw is not None:
                    qA.wait(sd, qw)
                qw = m.get(("l1vn", gn - 1, ti))
                if qw is not None:
                    qA.wait(ve, qw)
                qA.run(lambda e, bi=bi, b4=b4, ti=ti: e.activation(
                    x1t[bi][:], s1t[b4][:], AF.Square, accum_out=col(1, ti)))
                qA.run(lambda e, bi=bi, b4=b4, ti=ti: e.activation(
                    x1t[bi][:], s1t[b4][:], AF.Copy, accum_out=col(0, ti)),
                    inc=act)
                m["l1st", gn, ti] = act.n

                qV.wait(act, m["l1st", gn, ti])
                qw = m.get(("nb1", gn - 1, ti // 4))
                if qw is not None:
                    qV.wait(gp, qw)
                qV.run(lambda e, ti=ti: e.tensor_scalar(
                    col(2, ti), col(0, ti), -INV_D, None, ALU.mult), inc=ve)
                m["l1nm", gn, ti] = ve.n

                qG.wait(ve, m["l1nm", gn, ti])
                qG.run(lambda e, ti=ti: e.tensor_tensor(
                    col(3, ti), col(2, ti), col(2, ti), ALU.mult), inc=gp)
                m["l1pm", gn, ti] = gp.n

                qV.wait(gp, m["l1pm", gn, ti])
                qV.run(lambda e, ti=ti: e.scalar_tensor_tensor(
                    col(4, ti), col(1, ti), INV_D, col(3, ti), ALU.mult,
                    ALU.subtract), inc=ve)
                m["l1vn", gn, ti] = ve.n

                if ti % 4 == 3:
                    b = ti // 4
                    qA.wait(ve, m["l1vn", gn, ti])
                    qA.run(lambda e, b=b: e.activation(
                        rstats[:, 32 + 4 * b: 36 + 4 * b],
                        stats[:, 32 + 4 * b: 36 + 4 * b], AF.Sqrt,
                        bias=eps_sb[:]), inc=act)
                    m["sq1", gn, b] = act.n

                    qV.wait(act, m["sq1", gn, b])
                    def _rc1(e, b=b):
                        with nc.allow_low_precision(
                                reason="f32 reciprocal of LN std"):
                            return e.reciprocal(
                                rstats[:, 40 + 4 * b: 44 + 4 * b],
                                rstats[:, 32 + 4 * b: 36 + 4 * b])
                    qV.run(_rc1, inc=ve)
                    m["rc1", gn, b] = ve.n

                    qG.wait(ve, m["rc1", gn, b])
                    qG.run(lambda e, b=b: e.tensor_tensor(
                        rstats[:, 48 + 4 * b: 52 + 4 * b],
                        stats[:, 16 + 4 * b: 20 + 4 * b],
                        rstats[:, 40 + 4 * b: 44 + 4 * b], ALU.mult), inc=gp)
                    m["nb1", gn, b] = gp.n

            def emit_prefm(gn, tj):
                bj = tj % 2
                b4 = tj % 4
                qA.wait(gp, m["nb1", gn, tj // 4])
                qw = (m.get(("tp1", gn, tj - 2)) if tj >= 2
                      else m.get(("tp1", gn - 1, 6 + bj)))
                if qw is not None:
                    qA.wait(pe, qw)
                qw = (m.get(("x1w", gn, tj - 2)) if tj >= 2
                      else m.get(("x1w", gn - 1, 6 + bj)))
                if qw is not None:
                    qA.wait(sd, qw)
                qA.run(lambda e, bj=bj, b4=b4, tj=tj: e.activation(
                    x1t[bj][:], s1t[b4][:], AF.Identity,
                    scale=rcol(5, tj), bias=rcol(6, tj)), inc=act)
                m["l1ap", gn, tj] = act.n

                qSY.wait(act, m["l1ap", gn, tj])
                qSY.run(lambda e, gn=gn, tj=tj, bj=bj: e.dma_start(
                    x1_scr[gn * 8 + tj, :, :], x1t[bj][:]), inc=sd, k=16)
                m["x1w", gn, tj] = sd.n

                emit_pre_fm(gn, tj)

            def emit_pre_fm(gn, ti):
                bi = ti % 2
                gpar = gn % 2
                prev = (m.get(("scev", gn, ti - 1))
                        if ti >= 1 else m.get(("scev", gn - 1, 7)))
                for p in range(2):
                    qPE.wait(act, m["l1ap", gn, ti])
                    if p == 0:
                        if prev is not None:
                            qPE.wait(act, prev)
                    else:
                        qPE.wait(ve, m["lo", gn, ti, 0])
                    for c2 in range(4):
                        qPE.run(lambda e, bi=bi, p=p, c2=c2: e.transpose(
                            ps_pre[:, c2 * 128:(c2 + 1) * 128],
                            x1t[bi][:, (p * 4 + c2) * 128:
                                    (p * 4 + c2 + 1) * 128],
                            eye32_sb[:]),
                            inc=pe if c2 == 3 else None)
                    m["tp" + str(p), gn, ti] = pe.n

                    qA.wait(pe, m["tp" + str(p), gn, ti])

                    def _hi(e, gpar=gpar, p=p, ti=ti):
                        dst = x1fm[gpar][:].rearrange(
                            "q (c t2) -> q c t2", t2=1024)[
                            :, 4 * p:4 * p + 4, ti * 128:(ti + 1) * 128]
                        src = ps_pre[:].rearrange("q (c t2) -> q c t2", t2=128)
                        return e.copy(dst, src)
                    qA.run(_hi, inc=act)
                    m["hi", gn, ti, p] = act.n

                    qV.wait(act, m["hi", gn, ti, p])
                    w_ = (m.get(("scm", gn, ti - 1)) if ti >= 1
                          else m.get(("scm", gn - 1, 7)))
                    if w_ is not None and p == 0:
                        qV.wait(pe, w_)

                    def _lo(e, gpar=gpar, bi=bi, p=p, ti=ti):
                        dst = x1loT[:].rearrange(
                            "q (c t2) -> q c t2", t2=128)[:, 4 * p:4 * p + 4, :]
                        hi_ = x1fm[gpar][:].rearrange(
                            "q (c t2) -> q c t2", t2=1024)[
                            :, 4 * p:4 * p + 4, ti * 128:(ti + 1) * 128]
                        src = ps_pre[:].rearrange("q (c t2) -> q c t2", t2=128)
                        with nc.allow_low_precision(
                                reason="bf16 lo residual of x1 hi/lo split"):
                            return e.tensor_tensor(dst, src, hi_, ALU.subtract)
                    qV.run(_lo, inc=ve)
                    m["lo", gn, ti, p] = ve.n

                qPE.wait(ve, m["lo", gn, ti, 1])
                if gn == 0 and ti == 0:
                    qPE.wait(ve, m["glo"])
                k = 0
                for xa, ga in ((0, ghi), (1, ghi), (0, glo)):
                    for hc in range(8):
                        def _sc(e, gpar=gpar, bi=bi, ti=ti, xa=xa, ga=ga,
                                hc=hc, k=k):
                            lhs = (x1fm[gpar][:, hc * 1024 + ti * 128:
                                              hc * 1024 + (ti + 1) * 128]
                                   if xa == 0 else
                                   x1loT[:, hc * 128:(hc + 1) * 128])
                            return e.matmul(ps_pre[:, 0:16], lhs,
                                            ga[:, hc * 16:(hc + 1) * 16],
                                            start=(k == 0), stop=(k == 23))
                        qPE.run(_sc, inc=pe if k == 23 else None)
                        k += 1
                m["scm", gn, ti] = pe.n

                qA.wait(pe, m["scm", gn, ti])
                qA.run(lambda e, ti=ti: e.copy(S_all[:, ti * 16:(ti + 1) * 16],
                                               ps_pre[:, 0:16]), inc=act)
                m["scev", gn, ti] = act.n

            def emit_sigb(gn):
                w_ = m.get(("route", gn - 1, 7))
                if w_ is not None:
                    qA.wait(ve, w_)
                w_ = m.get(("rt8", gn - 1, 7))
                if w_ is not None:
                    qA.wait(gp, w_)
                qA.run(lambda e: e.activation(Ssig[:], S_all[:], AF.Sigmoid),
                       inc=act)
                m["sigb", gn] = act.n

            def emit_route_tile(gn, ti):
                z = slice(ti * 16, (ti + 1) * 16)
                qV.wait(act, m["sigb", gn])
                w_ = m.get(("rt3", gn, ti - 1))
                if w_ is not None:
                    qV.wait(gp, w_)
                qV.run(lambda e, ti=ti, z=z: e.reduce_max(
                    rtc(0, ti), Ssig[:, z], axis=AX.X), inc=ve)
                m["rt1", gn, ti] = ve.n

                qG.wait(ve, m["rt1", gn, ti])
                qG.run(lambda e, ti=ti: e.tensor_copy(rtc(5, ti), rtc(0, ti)),
                       inc=gp)
                m["rt1b", gn, ti] = gp.n

                qV.wait(gp, m["rt1b", gn, ti])
                qV.run(lambda e, ti=ti, z=z: e.tensor_scalar(
                    msk_sb[:], Ssig[:, z], rtc(5, ti), -1e9, ALU.is_ge,
                    ALU.mult), inc=ve)
                m["rt2", gn, ti] = ve.n

                qG.wait(ve, m["rt2", gn, ti])
                w_ = m.get(("rt4", gn, ti - 1))
                if w_ is not None:
                    qG.wait(ve, w_)
                qG.run(lambda e, z=z: e.tensor_tensor(
                    msk2_sb[:], msk_sb[:], Ssig[:, z], ALU.add), inc=gp)
                m["rt3", gn, ti] = gp.n

                qV.wait(gp, m["rt3", gn, ti])
                qV.run(lambda e, ti=ti: e.reduce_max(
                    rtc(1, ti), msk2_sb[:], axis=AX.X), inc=ve)
                m["rt4", gn, ti] = ve.n

                qG.wait(ve, m["rt4", gn, ti])
                qG.run(lambda e, ti=ti: e.tensor_copy(rtc(6, ti), rtc(1, ti)),
                       inc=gp)
                m["rt4b", gn, ti] = gp.n

                qA.wait(ve, m["rt4", gn, ti])
                qA.run(lambda e, ti=ti: e.activation(
                    rtc(2, ti), rtc(0, ti), AF.Identity, bias=rtc(1, ti)),
                    inc=act)
                m["rt5", gn, ti] = act.n

                qV.wait(act, m["rt5", gn, ti])
                qV.wait(gp, m["rt4b", gn, ti])
                w_ = m.get(("rt8", gn, ti - 1))
                if w_ is not None:
                    qV.wait(gp, w_)

                def _rt6(e, ti=ti, z=z):
                    e.tensor_scalar(ge2_sb[:], Ssig[:, z], rtc(6, ti), None,
                                    ALU.is_ge)
                    with nc.allow_low_precision(
                            reason="f32 reciprocal of routing denom"):
                        return e.reciprocal(rtc(3, ti), rtc(2, ti))
                qV.run(_rt6, inc=ve)
                m["rt6", gn, ti] = ve.n

                qA.wait(ve, m["rt6", gn, ti])
                qA.run(lambda e, ti=ti: e.activation(
                    rtc(4, ti), rtc(3, ti), AF.Identity, scale=2.5), inc=act)
                m["rt7", gn, ti] = act.n

                qG.wait(ve, m["rt6", gn, ti])
                w_ = m.get(("route", gn, ti - 1))
                if w_ is not None:
                    qG.wait(ve, w_)
                qG.run(lambda e, z=z: e.tensor_tensor(
                    sg_sb[:], Ssig[:, z], ge2_sb[:], ALU.mult), inc=gp)
                m["rt8", gn, ti] = gp.n

                qV.wait(gp, m["rt8", gn, ti])
                qV.wait(act, m["rt7", gn, ti])
                qV.run(lambda e, gn=gn, ti=ti: e.tensor_scalar(
                    route_g[gn % 2][:, ti * 16:(ti + 1) * 16], sg_sb[:],
                    rtc(4, ti), None, ALU.mult), inc=ve)
                m["route", gn, ti] = ve.n

            def emit_v2(g, ti):
                b4 = ti % 4
                qw = (m.get(("l2ap", g, ti - 4)) if ti >= 4
                      else m.get(("l2ap", g - 1, 4 + ti)))
                if qw is not None:
                    qSY.wait(act, qw)
                qSY.run(lambda e, g=g, ti=ti, b4=b4: e.dma_start(
                    x1v[b4][:], x1_scr[g * 8 + ti, :, :]), inc=sd, k=16)
                m["x1vld", g, ti] = sd.n

                qG.wait(sd, m["x1vld", g, ti])
                ge_sh = g * 17 + NE
                qG.wait(ve, m["dn", ge_sh, ti // 4, ti % 4])
                qG.run(lambda e, g=g, ti=ti, b4=b4: e.tensor_tensor(
                    x1v[b4][:], x1v[b4][:],
                    acc_g[:, ti * 1024:(ti + 1) * 1024], ALU.add), inc=gp)
                m["v2", g, ti] = gp.n

            def emit_ln2_stats(g, ti):
                bi2 = ti % 2
                b4 = ti % 4
                qA.wait(gp, m["v2", g, ti])
                qw = (m.get(("outw", g * 8 + ti - 2))
                      or m.get(("outw", (g - 1) * 8 + ti + 6)))
                if qw is not None:
                    qA.wait(sd, qw)
                qw = m.get(("l2vn", g - 1, ti))
                if qw is not None:
                    qA.wait(ve, qw)
                qA.run(lambda e, ti=ti, bi2=bi2, b4=b4: e.activation(
                    ot[bi2][:], x1v[b4][:], AF.Square, accum_out=col(9, ti)))
                qA.run(lambda e, ti=ti, bi2=bi2, b4=b4: e.activation(
                    ot[bi2][:], x1v[b4][:], AF.Copy, accum_out=col(8, ti)),
                    inc=act)
                m["l2st", g, ti] = act.n

                qV.wait(act, m["l2st", g, ti])
                qw = m.get(("nb2", g - 1, ti // 4))
                if qw is not None:
                    qV.wait(gp, qw)
                qV.run(lambda e, ti=ti: e.tensor_scalar(
                    col(10, ti), col(8, ti), -INV_D, None, ALU.mult), inc=ve)
                m["l2nm", g, ti] = ve.n

                qG.wait(ve, m["l2nm", g, ti])
                qG.run(lambda e, ti=ti: e.tensor_tensor(
                    col(11, ti), col(10, ti), col(10, ti), ALU.mult), inc=gp)
                m["l2pm", g, ti] = gp.n

                qV.wait(gp, m["l2pm", g, ti])
                qV.run(lambda e, ti=ti: e.scalar_tensor_tensor(
                    col(12, ti), col(9, ti), INV_D, col(11, ti), ALU.mult,
                    ALU.subtract), inc=ve)
                m["l2vn", g, ti] = ve.n

            def emit_ln2_apply(g, b):
                qA.wait(ve, m["l2vn", g, 4 * b + 3])
                qA.run(lambda e, b=b: e.activation(
                    rstats[:, 96 + 4 * b: 100 + 4 * b],
                    stats[:, 96 + 4 * b: 100 + 4 * b], AF.Sqrt,
                    bias=eps_sb[:]), inc=act)
                m["sq2", g, b] = act.n

                qV.wait(act, m["sq2", g, b])

                def _rc2(e, b=b):
                    with nc.allow_low_precision(
                            reason="f32 reciprocal of LN std"):
                        return e.reciprocal(rstats[:, 104 + 4 * b: 108 + 4 * b],
                                            rstats[:, 96 + 4 * b: 100 + 4 * b])
                qV.run(_rc2, inc=ve)
                m["rc2", g, b] = ve.n

                qG.wait(ve, m["rc2", g, b])
                qG.run(lambda e, b=b: e.tensor_tensor(
                    rstats[:, 112 + 4 * b: 116 + 4 * b],
                    stats[:, 80 + 4 * b: 84 + 4 * b],
                    rstats[:, 104 + 4 * b: 108 + 4 * b], ALU.mult), inc=gp)
                m["nb2", g, b] = gp.n

                for tj in range(4 * b, 4 * b + 4):
                    bj = tj % 2
                    qA.wait(gp, m["nb2", g, b])
                    if tj >= 2:
                        qA.wait(sd, m["outw", g * 8 + tj - 2])
                    qA.run(lambda e, g=g, tj=tj, bj=bj: e.activation(
                        ot[bj][:], x1v[tj % 4][:], AF.Identity,
                        scale=rstats[:, 104 + tj: 105 + tj],
                        bias=rstats[:, 112 + tj: 113 + tj]), inc=act)
                    m["l2ap", g, tj] = act.n

                    qSY.wait(act, m["l2ap", g, tj])
                    qSY.run(lambda e, g=g, tj=tj, bj=bj: e.dma_start(
                        out[(g * 8 + tj) * 128:(g * 8 + tj + 1) * 128, :],
                        ot[bj][:]), inc=sd, k=16)
                    m["outw", g * 8 + tj] = sd.n


            if DEV_LIMIT["pre"]:
                for ti in range(4):
                    emit_pre_tile(0, ti)
                for ti in range(4):
                    emit_prefm(0, ti)
                for ti in range(4, 8):
                    emit_pre_tile(0, ti)
                for ti in range(4, 8):
                    emit_prefm(0, ti)
                emit_sigb(0)
                for ti in range(GT):
                    emit_route_tile(0, ti)

            load_w(qG, 0, 0, 0)
            m["wld", 0] = dw.n

            for g in range(NGRP):
                tasks = []
                if g >= 1:
                    tasks.append(("ln2statsA", g - 1, 0))
                    tasks.append(("ln2mid", g - 1, 0))
                    tasks.append(("ln2statsB", g - 1, 0))
                    tasks.append(("ln2fin", g - 1, 0))
                if g + 1 < NGRP and DEV_LIMIT["pre"]:
                    for ti in range(4):
                        tasks.append(("pre", g + 1, ti))
                    for ti in range(4):
                        tasks.append(("prefm", g + 1, ti))
                    for ti in range(4, 8):
                        tasks.append(("pre", g + 1, ti))
                    for ti in range(4, 8):
                        tasks.append(("prefm", g + 1, ti))
                    tasks.append(("sigbr", g + 1, 0))

                if g >= 1:
                    for ti in range(4):
                        emit_v2(g - 1, ti)

                def run_task(t):
                    kind, a, bx = t
                    if kind == "ln2statsA":
                        for ti in range(4):
                            emit_ln2_stats(a, ti)
                    elif kind == "ln2mid":
                        emit_ln2_apply(a, 0)
                        for ti in range(4, 8):
                            emit_v2(a, ti)
                    elif kind == "ln2statsB":
                        for ti in range(4, 8):
                            emit_ln2_stats(a, ti)
                    elif kind == "ln2fin":
                        emit_ln2_apply(a, 1)
                    elif kind == "pre":
                        emit_pre_tile(a, bx)
                    elif kind == "prefm":
                        emit_prefm(a, bx)
                    elif kind == "sigbr":
                        emit_sigb(a)
                        for ti in range(GT):
                            emit_route_tile(a, ti)

                slot = 0
                for e_ in range(NEX):
                    ge = g * 17 + e_
                    if e_ + 1 < NEX:
                        if ge >= 1:
                            qG.wait(pe, m["edone", ge - 1])
                        load_w(qG, g, e_ + 1, (ge + 1) % 2)
                        m["wld", ge + 1] = dw.n
                    elif g + 1 < NGRP:
                        qG.wait(pe, m["edone", ge - 1])
                        load_w(qG, g + 1, 0, (ge + 1) % 2)
                        m["wld", ge + 1] = dw.n

                    for c in range(2):
                        for mc in range(4):
                            qPE.wait(dw, m["wld", ge])
                            if e_ == 0 and c == 0 and mc == 0:
                                qPE.wait(ve, m["route", g, GT - 1])
                                qPE.wait(act, m["hi", g, 7, 1])
                            if len(silu_hist) >= 2:
                                qPE.wait(act, silu_hist[-2])
                            for hc in range(8):
                                qPE.run(lambda e, ge=ge, g=g, c=c, mc=mc, hc=hc:
                                        e.matmul(
                                            ps_g[mc % 2][:],
                                            wg[ge % 2][:, hc * 512 + mc * 128: hc * 512 + (mc + 1) * 128],
                                            x1fm[g % 2][:, hc * 1024 + c * 512: hc * 1024 + (c + 1) * 512],
                                            start=(hc == 0), stop=(hc == 7)),
                                        inc=pe if hc == 7 else None)
                            m["gmm", ge, c, mc] = pe.n

                            qA.wait(pe, m["gmm", ge, c, mc])
                            if len(gu_hist) >= 2:
                                qA.wait(ve, gu_hist[-2])
                            qA.run(lambda e, mc=mc: e.activation(
                                gs[mc % 2][:], ps_g[mc % 2][:], AF.Silu),
                                inc=act)
                            silu_hist.append(act.n)
                            m["silu", ge, c, mc] = act.n

                            if gu_hist:
                                qPE.wait(ve, gu_hist[-1])
                            for hc in range(8):
                                qPE.run(lambda e, ge=ge, g=g, c=c, mc=mc, hc=hc:
                                        e.matmul(
                                            ps_u[:],
                                            wu[ge % 2][:, hc * 512 + mc * 128: hc * 512 + (mc + 1) * 128],
                                            x1fm[g % 2][:, hc * 1024 + c * 512: hc * 1024 + (c + 1) * 512],
                                            start=(hc == 0), stop=(hc == 7)),
                                        inc=pe if hc == 7 else None)
                            m["umm", ge, c, mc] = pe.n

                            qV.wait(act, m["silu", ge, c, mc])
                            qV.wait(pe, m["umm", ge, c, mc])
                            sec = ge * 2 + c
                            if mc == 0 and ("dnmm", sec - 2) in m:
                                qV.wait(pe, m["dnmm", sec - 2])
                            qV.run(lambda e, mc=mc, sec=sec: e.tensor_tensor(
                                guT[sec % 2][:, mc * 512:(mc + 1) * 512],
                                gs[mc % 2][:], ps_u[:], ALU.mult), inc=ve)
                            gu_hist.append(ve.n)
                            m["gu", ge, c, mc] = ve.n

                        if tasks:
                            run_task(tasks.pop(0))
                        for ti2 in range(4):
                            qPE.wait(ve, m["gu", ge, c, 3])
                            if len(dn_hist) >= 2:
                                qPE.wait(ve, dn_hist[-2])
                            sec = ge * 2 + c
                            for mc in range(4):
                                for nh in range(2):
                                    qPE.run(lambda e, ge=ge, mc=mc, nh=nh,
                                            ti2=ti2, sec=sec: e.matmul(
                                                ps_dn[ti2 % 2][:, nh * 512:(nh + 1) * 512],
                                                guT[sec % 2][:, mc * 512 + ti2 * 128: mc * 512 + (ti2 + 1) * 128],
                                                wd[ge % 2][:, mc * 1024 + nh * 512: mc * 1024 + (nh + 1) * 512],
                                                start=(mc == 0), stop=(mc == 3)),
                                            inc=pe if (mc == 3 and nh == 1)
                                            else None)
                            dmm_hist.append(pe.n)
                            m["dmm", ge, c, ti2] = pe.n
                            if ti2 == 3:
                                m["dnmm", sec] = pe.n

                            qV.wait(pe, m["dmm", ge, c, ti2])
                            if e_ == 0 and g >= 1:
                                qV.wait(gp, m["v2", g - 1, c * 4 + ti2])

                            def _acc(e, g=g, e_=e_, c=c, ti2=ti2):
                                ti = c * 4 + ti2
                                accs = acc_g[:, ti * 1024:(ti + 1) * 1024]
                                if e_ == 0:
                                    rc = route_g[g % 2][:, ti * 16: ti * 16 + 1]
                                    return e.tensor_scalar(
                                        accs, ps_dn[ti2 % 2][:], rc, None,
                                        ALU.mult)
                                elif e_ < NE:
                                    rc = route_g[g % 2][:, ti * 16 + e_:
                                                        ti * 16 + e_ + 1]
                                    return e.scalar_tensor_tensor(
                                        accs, ps_dn[ti2 % 2][:], rc, accs,
                                        ALU.mult, ALU.add)
                                return e.tensor_tensor(accs, ps_dn[ti2 % 2][:],
                                                       accs, ALU.add)
                            qV.run(_acc, inc=ve)
                            dn_hist.append(ve.n)
                            m["dn", ge, c, ti2] = ve.n
                    m["edone", ge] = pe.n

            if DEV_LIMIT["ln2"]:
                gl = NGRP - 1
                for ti in range(4):
                    emit_v2(gl, ti)
                emit_ln2_stats(gl, 0)
                emit_ln2_stats(gl, 1)
                emit_ln2_stats(gl, 2)
                emit_ln2_stats(gl, 3)
                emit_ln2_apply(gl, 0)
                for ti in range(4, 8):
                    emit_v2(gl, ti)
                for ti in range(4, 8):
                    emit_ln2_stats(gl, ti)
                emit_ln2_apply(gl, 1)

            qSY.wait(sd)
            qSY.flush(blk.sync)
            qPE.flush(blk.tensor)
            qV.flush(blk.vector)
            qA.flush(blk.scalar)
            qG.flush(blk.gpsimd)

    with nc.Block(no_gpsimd_drain=True) as blk:
        @blk.sync
        def _(q):
            if sd.n:
                q.wait_ge(sd.h, sd.n)
            if dw.n:
                q.wait_ge(dw.h, dw.n)
    return nc


def prep_inputs(inputs):
    x = np.ascontiguousarray(np.asarray(inputs["L0_attn_x"], dtype=np.float32))
    w_qkvT = np.ascontiguousarray(
        np.asarray(inputs["L0_attn_w_qkv"], dtype=np.float32).T)
    w_projT = np.ascontiguousarray(
        np.asarray(inputs["L0_attn_w_proj"], dtype=np.float32).T)
    b_proj = np.ascontiguousarray(
        np.asarray(inputs["L0_attn_b_proj"], dtype=np.float32).reshape(1, D))
    gate_w = np.asarray(inputs["L0_ffn_gate_w"], dtype=np.float32)
    gate_pk = np.ascontiguousarray(
        gate_w.T.reshape(8, 128, 16).transpose(1, 0, 2).reshape(128, 128))
    w_gate = np.ascontiguousarray(np.asarray(inputs["L0_ffn_w_gate"], np.float32))
    w_up = np.ascontiguousarray(np.asarray(inputs["L0_ffn_w_up"], np.float32))
    w_down = np.ascontiguousarray(np.asarray(inputs["L0_ffn_w_down"], np.float32))
    ws_gate = np.ascontiguousarray(np.asarray(inputs["L0_ffn_ws_gate"], np.float32))
    ws_up = np.ascontiguousarray(np.asarray(inputs["L0_ffn_ws_up"], np.float32))
    ws_down = np.ascontiguousarray(np.asarray(inputs["L0_ffn_ws_down"], np.float32))

    inv_freq = (1.0 / 10000.0 ** (np.arange(0, HD, 2, dtype=np.float32) / HD)
                ).astype(np.float32)
    t = np.arange(N, dtype=np.float32)
    freqs = (t[:, None] * inv_freq[None, :]).astype(np.float32)
    cos_pk = np.ascontiguousarray(np.cos(freqs).astype(np.float32).reshape(
        NT, 128, 32).transpose(1, 0, 2).reshape(128, NT * 32))
    sin_pk = np.ascontiguousarray(np.sin(freqs).astype(np.float32).reshape(
        NT, 128, 32).transpose(1, 0, 2).reshape(128, NT * 32))
    eye = np.eye(128, dtype=ml_dtypes.bfloat16)

    shared = dict(w_qkvT=w_qkvT, w_projT=w_projT, b_proj=b_proj, gate_pk=gate_pk,
                  w_gate=w_gate, w_up=w_up, w_down=w_down, ws_gate=ws_gate,
                  ws_up=ws_up, ws_down=ws_down, cos_pk=cos_pk, sin_pk=sin_pk,
                  eye_in=eye, eye32_in=np.eye(128, dtype=np.float32))
    return [dict(shared, x=np.ascontiguousarray(x[c])) for c in range(B)]


def kernel(**inputs):
    in_maps = prep_inputs(inputs)
    nc = build_program()
    res = run_bass_kernel_spmd(nc, in_maps, core_ids=list(range(B)))
    return np.stack([np.asarray(res.results[c]["out"]) for c in range(B)], axis=0)


# version 37 (final)
# speedup vs baseline: 1.1927x; 1.1927x over previous
from contextlib import ExitStack

import numpy as np
import ml_dtypes
import concourse.bass as bass
import concourse.mybir as mybir
from concourse.bass_utils import run_bass_kernel_spmd

F32 = mybir.dt.float32
BF16 = mybir.dt.bfloat16
AF = mybir.ActivationFunctionType
ALU = mybir.AluOpType
AX = mybir.AxisListType

B, N, D = 8, 4096, 1024
H, HD = 16, 64
NT = N // 128
NE, M = 16, 512
NQB = 8
NPAIR = 8
NG = 4
GT = 8
EPS = 1e-5
INV_D = 1.0 / 1024.0

DEV_LIMIT = {"qb": NQB, "grp": NG, "nt": NT, "ne": NE + 1, "pre": True,
             "ln2": True}


class Sem:
    def __init__(self, nc, name):
        self.h = nc.alloc_semaphore(name)
        self.n = 0


class Dq:

    def __init__(self):
        self.items = []

    def wait(self, sem, val=None):
        v = sem.n if val is None else val
        if v > 0:
            self.items.append(lambda e, h=sem.h, v=v: e.wait_ge(h, v))

    def run(self, fn, inc=None, k=1):
        if inc is not None:
            inc.n += k
            self.items.append(
                lambda e, fn=fn, h=inc.h, k=k: fn(e).then_inc(h, k))
            return inc.n
        self.items.append(fn)
        return None

    def flush(self, section):
        items, self.items = self.items, []
        if not items:
            return

        @section
        def _(eng):
            for it in items:
                it(eng)


def build_program(dev_dump=()):
    nc = bass.Bass("TRN2", target_bir_lowering=False)

    x_in = nc.dram_tensor("x", [N, D], F32, kind="ExternalInput").ap()
    w_qkvT = nc.dram_tensor("w_qkvT", [D, 3 * D], F32, kind="ExternalInput").ap()
    w_projT = nc.dram_tensor("w_projT", [D, D], F32, kind="ExternalInput").ap()
    b_proj = nc.dram_tensor("b_proj", [1, D], F32, kind="ExternalInput").ap()
    gate_pk = nc.dram_tensor("gate_pk", [128, 128], F32, kind="ExternalInput").ap()
    w_gate = nc.dram_tensor("w_gate", [NE, D, M], F32, kind="ExternalInput").ap()
    w_up = nc.dram_tensor("w_up", [NE, D, M], F32, kind="ExternalInput").ap()
    w_down = nc.dram_tensor("w_down", [NE, M, D], F32, kind="ExternalInput").ap()
    ws_gate = nc.dram_tensor("ws_gate", [D, M], F32, kind="ExternalInput").ap()
    ws_up = nc.dram_tensor("ws_up", [D, M], F32, kind="ExternalInput").ap()
    ws_down = nc.dram_tensor("ws_down", [M, D], F32, kind="ExternalInput").ap()
    cos_pk = nc.dram_tensor("cos_pk", [128, NT * 32], F32, kind="ExternalInput").ap()
    sin_pk = nc.dram_tensor("sin_pk", [128, NT * 32], F32, kind="ExternalInput").ap()
    eye_in = nc.dram_tensor("eye_in", [128, 128], BF16, kind="ExternalInput").ap()
    eye32_in = nc.dram_tensor("eye32_in", [128, 128], F32,
                              kind="ExternalInput").ap()

    def scratch(name, shape, dtype):
        kind = "ExternalOutput" if name in dev_dump else "Internal"
        return nc.dram_tensor(name, shape, dtype, kind=kind).ap()

    q_scr = scratch("q_scr", [NT, 128, 1024], BF16)
    k_scr = scratch("k_scr", [NPAIR, 128, N], BF16)
    v_scr = scratch("v_scr", [NPAIR, 128, N], BF16)
    s1_scr = scratch("s1_scr", [NT, 128, D], F32)
    x1_scr = scratch("x1_scr", [NT, 128, D], F32)
    out = nc.dram_tensor("out", [N, D], F32, kind="ExternalOutput").ap()

    sd = Sem(nc, "sd")
    dw = Sem(nc, "dw")
    pe = Sem(nc, "pe")
    ve = Sem(nc, "ve")
    act = Sem(nc, "act")
    gp = Sem(nc, "gp")
    m = {}

    ones_sb = nc.alloc_sbuf_tensor("ones_sb", [128, 128], BF16)
    eye_sb = nc.alloc_sbuf_tensor("eye_sb", [128, 128], BF16)

    with ExitStack() as _st:
        _sb = lambda n_, s_, d_: _st.enter_context(nc.sbuf_tensor(n_, s_, d_))
        _ps = lambda n_, s_, d_: _st.enter_context(nc.psum_tensor(n_, s_, d_))
        w_all = _sb("w_all", [128, 8 * 3072], BF16)
        cos_sb = _sb("cos_sb", [128, NT * 32], F32)
        sin_sb = _sb("sin_sb", [128, NT * 32], F32)
        x_bf = [_sb(f"x_bf{i}", [128, D], BF16) for i in range(4)]
        x_fm = [_sb(f"x_fm{i}", [128, D], BF16) for i in range(4)]
        rt = [_sb(f"rt{i}", [128, 256], F32) for i in range(16)]
        roped = [_sb(f"roped{i}", [128, 512], BF16) for i in range(4)]
        q_t = [_sb(f"q_t{i}", [128, 1024], BF16) for i in range(3)]
        k_t = [_sb(f"k_t{i}", [128, 1024], BF16) for i in range(3)]
        v_t = [_sb(f"v_t{i}", [128, 1024], BF16) for i in range(3)]
        ps_x = [_ps(f"ps_xr{i}", [128, 1024], BF16) for i in range(2)]
        ps_qk = [_ps(f"ps_qk{i}", [128, 512], F32) for i in range(3)]
        ps_t = [_ps(f"ps_t{i}", [128, 512], BF16) for i in range(3)]
        blk = _st.enter_context(nc.Block(no_gpsimd_drain=True))
        qSY, qPE, qV, qA, qG = Dq(), Dq(), Dq(), Dq(), Dq()

        qSY.run(lambda e: e.dma_start(cos_sb[:], cos_pk), inc=sd, k=16)
        qSY.run(lambda e: e.dma_start(sin_sb[:], sin_pk), inc=sd, k=16)
        qSY.run(lambda e: e.dma_start(eye_sb[:], eye_in), inc=sd, k=16)
        m["const"] = sd.n

        for ec in range(6):
            def _ldw(e, ec=ec):
                src = w_qkvT[:, ec * 512:(ec + 1) * 512].rearrange(
                    "(c p) e2 -> p c e2", p=128)
                dst = w_all[:].rearrange("p (c e2) -> p c e2", e2=3072)[
                    :, :, ec * 512:(ec + 1) * 512]
                return e.dma_start(dst, src)
            qG.run(_ldw, inc=dw, k=16)
        m["w_qkv"] = dw.n

        qV.wait(sd, m["const"])
        qV.run(lambda e: e.memset(ones_sb[:], 1.0), inc=ve)
        m["ones"] = ve.n

        pend_tev = {}
        pend_rT = []
        pend_comb = []
        vei = [0]
        rope_vei = {}

        def try_flush_combs():
            while pend_comb and vei[0] - rope_vei[pend_comb[0]] >= 10:
                emit_comb(pend_comb.pop(0))

        def emit_comb(qkc, on_gp=False):
            rb = (qkc % 4) * 4
            r32 = lambda tt: tt[:].rearrange("p (h f) -> p h f", f=32)

            def _comb(e, qkc=qkc, rb=rb, r32=r32):
                rv = roped[qkc % 4][:].rearrange("p (h f) -> p h f", f=64)
                e.tensor_tensor(rv[:, :, 0:32], r32(rt[rb]),
                                r32(rt[rb + 1]), ALU.subtract)
                return e.tensor_tensor(rv[:, :, 32:64], r32(rt[rb + 2]),
                                       r32(rt[rb + 3]), ALU.add)
            if on_gp:
                qG.wait(ve, m["rope_q", qkc])
                if ("ropeTq", qkc - 4) in m:
                    qG.wait(pe, m["ropeTq", qkc - 4])
                qG.run(_comb, inc=gp)
                m["comb", qkc] = ("gp", gp.n)
            else:
                if ("ropeTq", qkc - 4) in m:
                    qV.wait(pe, m["ropeTq", qkc - 4])
                qV.run(_comb, inc=ve)
                vei[0] += 2
                m["comb", qkc] = ("ve", ve.n)

        def emit_ropeT(qkc):
            ceng, cnt = m["comb", qkc]
            qPE.wait(ve if ceng == "ve" else gp, cnt)
            if ("tevq", qkc - 3) in m:
                qPE.wait(ve, m["tevq", qkc - 3])
            for j in range(4):
                qPE.run(lambda e, qkc=qkc, j=j: e.transpose(
                    ps_t[qkc % 3][:, j * 128:(j + 1) * 128],
                    roped[qkc % 4][:, j * 128:(j + 1) * 128],
                    eye_sb[:]),
                    inc=pe if j == 3 else None)
            m["ropeTq", qkc] = pe.n

        def emit_tev(qkc):
            b3_, c_, gc_ = pend_tev.pop(qkc)
            qV.wait(pe, m["ropeTq", qkc])
            T_ = qkc // 4
            if T_ >= 3:
                qV.wait(sd, m["scrout", T_ - 3])

            def _tev(e, b3_=b3_, c_=c_, qkc=qkc):
                if c_ < 2:
                    dst = q_t[b3_][:, c_ * 512:(c_ + 1) * 512]
                else:
                    dst = k_t[b3_][:, (c_ - 2) * 512:(c_ - 1) * 512]
                return e.tensor_copy(dst, ps_t[qkc % 3][:])
            qV.run(_tev, inc=ve)
            vei[0] += 1
            m["tevq", qkc] = ve.n
            m["tev", gc_] = ve.n
            try_flush_combs()

        for T in range(DEV_LIMIT["nt"]):
            bi = T % 4
            b3 = T % 3
            if T >= 4:
                qG.wait(pe, m["xT", T - 4])
            qG.run(lambda e, T=T, bi=bi: e.dma_start(
                x_bf[bi][:], x_in[T * 128:(T + 1) * 128, :]), inc=dw, k=16)
            m["xcast", T] = dw.n

            qPE.wait(dw, m["xcast", T])
            if T >= 2:
                qPE.wait(ve, m["xfm_evac", T - 2])
            for c2 in range(8):
                qPE.run(lambda e, bi=bi, c2=c2, T=T: e.transpose(
                    ps_x[T % 2][:, c2 * 128:(c2 + 1) * 128],
                    x_bf[bi][:, c2 * 128:(c2 + 1) * 128], eye_sb[:]),
                    inc=pe if c2 == 7 else None)
            m["xT", T] = pe.n

            qV.wait(pe, m["xT", T])
            if T >= 4:
                qV.wait(pe, m["mm", T - 4, 5])
            qV.run(lambda e, bi=bi, T=T: e.tensor_copy(
                x_fm[bi][:], ps_x[T % 2][:]), inc=ve)
            vei[0] += 1
            m["xfm_evac", T] = ve.n
            try_flush_combs()

            for c in range(6):
                gc = T * 6 + c
                pq = gc % 3
                qPE.wait(ve, m["xfm_evac", T])
                if T == 0 and c == 0:
                    qPE.wait(dw, m["w_qkv"])
                if ("chunk_done", gc - 3) in m:
                    qPE.wait(ve, m["chunk_done", gc - 3])
                for hc in range(8):
                    qPE.run(lambda e, bi=bi, c=c, hc=hc, pq=pq: e.matmul(
                        ps_qk[pq][:],
                        x_fm[bi][:, hc * 128:(hc + 1) * 128],
                        w_all[:, hc * 3072 + c * 512: hc * 3072 + (c + 1) * 512],
                        start=(hc == 0), stop=(hc == 7)),
                        inc=pe if hc == 7 else None)
                m["mm", T, c] = pe.n

                r32 = lambda tt: tt[:].rearrange("p (h f) -> p h f", f=32)
                if c < 4:
                    qkc = T * 4 + c
                    rb = (qkc % 4) * 4
                    qV.wait(pe, m["mm", T, c])
                    if ("comb", qkc - 4) in m:
                        ce_, cn_ = m["comb", qkc - 4]
                        qV.wait(ve if ce_ == "ve" else gp, cn_)

                    def _rope(e, T=T, pq=pq, rb=rb):
                        p8 = ps_qk[pq][:].rearrange("p (h f) -> p h f", f=64)
                        x1h, x2h = p8[:, :, 0:32], p8[:, :, 32:64]
                        cb = cos_sb[:, T * 32:(T + 1) * 32].unsqueeze(1
                                    ).broadcast_to([128, 8, 32])
                        sn = sin_sb[:, T * 32:(T + 1) * 32].unsqueeze(1
                                    ).broadcast_to([128, 8, 32])
                        e.tensor_tensor(r32(rt[rb]), x1h, cb, ALU.mult)
                        e.tensor_tensor(r32(rt[rb + 1]), x2h, sn, ALU.mult)
                        e.tensor_tensor(r32(rt[rb + 2]), x2h, cb, ALU.mult)
                        return e.tensor_tensor(r32(rt[rb + 3]), x1h, sn, ALU.mult)
                    qV.run(_rope, inc=ve)
                    vei[0] += 4
                    m["rope", gc] = ve.n
                    m["rope_q", qkc] = ve.n
                    rope_vei[qkc] = vei[0]
                    m["chunk_done", gc] = ve.n

                    pend_comb.append(qkc)
                    try_flush_combs()

                    pend_rT.append(qkc)
                    pend_tev[qkc] = (b3, c, gc)
                else:
                    qV.wait(pe, m["mm", T, c])
                    if T >= 3:
                        qV.wait(sd, m["scrout", T - 3])
                    qV.run(lambda e, b3=b3, c=c, pq=pq: e.tensor_copy(
                        v_t[b3][:, (c - 4) * 512:(c - 3) * 512], ps_qk[pq][:]),
                        inc=ve)
                    vei[0] += 1
                    m["tev", gc] = ve.n
                    m["chunk_done", gc] = ve.n
                    try_flush_combs()

                if pend_rT and (len(pend_rT) >= 3 or c >= 4):
                    qk_f = pend_rT.pop(0)
                    while pend_comb and pend_comb[0] <= qk_f:
                        emit_comb(pend_comb.pop(0), on_gp=True)
                    emit_ropeT(qk_f)
                    if qk_f - 1 in pend_tev:
                        emit_tev(qk_f - 1)

            while pend_rT:
                qk_f = pend_rT.pop(0)
                while pend_comb and pend_comb[0] <= qk_f:
                    emit_comb(pend_comb.pop(0), on_gp=True)
                emit_ropeT(qk_f)
                if qk_f - 1 in pend_tev:
                    emit_tev(qk_f - 1)
            for qkc_ in (T * 4 + 2, T * 4 + 3):
                if qkc_ in pend_tev:
                    emit_tev(qkc_)
            qSY.wait(ve, m["tev", T * 6 + 5])
            qSY.wait(ve, m["tevq", T * 4 + 3])
            qSY.run(lambda e, T=T, b3=b3: e.dma_start(
                q_scr[T, :, :], q_t[b3][:]), inc=sd, k=16)

            def _kout(e, T=T, b3=b3):
                ksrc = k_t[b3][:].rearrange("q (r f) -> q r f", f=128)
                kdst = k_scr[:, :, T * 128:(T + 1) * 128].transpose([1, 0, 2])
                return e.dma_start(kdst, ksrc)
            qSY.run(_kout, inc=sd, k=16)

            def _vout(e, T=T, b3=b3):
                vsrc = v_t[b3][:].rearrange("q (r f) -> q r f", f=128)
                vdst = v_scr[:, :, T * 128:(T + 1) * 128].transpose([1, 0, 2])
                return e.dma_start(vdst, vsrc)
            qSY.run(_vout, inc=sd, k=16)
            m["scrout", T] = sd.n

        qSY.wait(sd)
        qSY.flush(blk.sync)
        qPE.flush(blk.tensor)
        qV.flush(blk.vector)
        qA.flush(blk.scalar)
        qG.flush(blk.gpsimd)

    with ExitStack() as _st:
        _sb = lambda n_, s_, d_: _st.enter_context(nc.sbuf_tensor(n_, s_, d_))
        _ps = lambda n_, s_, d_: _st.enter_context(nc.psum_tensor(n_, s_, d_))
        w_proj_sb = _sb("w_proj_sb", [128, 8 * 1024], BF16)
        kp = [_sb(f"kp{i}", [128, N], BF16) for i in range(3)]
        vp = [_sb(f"vp{i}", [128, N], BF16) for i in range(3)]
        pt = [_sb(f"pt{i}", [128, 8 * 1024], BF16) for i in range(5)]
        q_qb = [_sb(f"q_qb{i}", [128, 4096], BF16) for i in range(2)]
        o_all = [_sb(f"o_all{i}", [128, 8 * 512], BF16) for i in range(2)]
        rrow = _sb("rrow", [128, 512], BF16)
        r_sb = _sb("r_sb", [128, 512], BF16)
        xres = [_sb(f"xres{i}", [128, D], F32) for i in range(2)]
        ps_s = [_ps(f"ps_s{i}", [128, 1024], F32) for i in range(2)]
        ps_oA = _ps("ps_oA", [128, 512], F32)
        ps_oB = _ps("ps_oB", [128, 512], F32)
        ps_dA = _ps("ps_dA", [128, 512], F32)
        ps_dB = _ps("ps_dB", [128, 512], F32)
        qr = [q_qb[i][:].rearrange("p (t r f) -> p t r f", r=8, f=128)
              for i in range(2)]

        with nc.Block(no_gpsimd_drain=True) as blk:
            @blk.gpsimd
            def _(g):
                src = w_projT[:].rearrange("(c p) e2 -> p c e2", p=128)
                dst = w_proj_sb[:].rearrange("p (c e2) -> p c e2", e2=1024)
                g.dma_start(dst, src).then_inc(dw.h, 16)
                dw.n += 16
            m["w_proj"] = dw.n

        NJ = DEV_LIMIT["qb"] * NPAIR
        with nc.Block(no_gpsimd_drain=True) as blk:
            qSY, qPE, qV, qA = Dq(), Dq(), Dq(), Dq()

            def emit_qload(qb):
                if qb >= DEV_LIMIT["qb"]:
                    return
                if qb >= 2:
                    qPE_mark = m["Sdone", (qb - 2) * 8 + 7]
                    qSY.wait(pe, qPE_mark)
                for i in range(4):
                    qSY.run(lambda e, qb=qb, i=i: e.dma_start(
                        q_qb[qb % 2][:, i * 1024:(i + 1) * 1024],
                        q_scr[qb * 4 + i, :, :]), inc=sd, k=16)
                m["qld", qb] = sd.n

            def emit_prefetch(j):
                if j + 2 >= NJ:
                    return
                qSY.wait(pe, m["Sdone", j - 1])
                qSY.wait(pe, m["odone", j - 1])
                qSY.run(lambda e, j=j: e.dma_start(
                    kp[(j + 2) % 3][:], k_scr[(j + 2) % 8, :, :]), inc=sd, k=16)
                qSY.run(lambda e, j=j: e.dma_start(
                    vp[(j + 2) % 3][:], v_scr[(j + 2) % 8, :, :]), inc=sd, k=16)
                m["kv", j + 2] = sd.n

            def emit_s_kt(j, s4, kt, act_pb):
                qb = j // 8
                if s4 == 0 and kt == 0:
                    qPE.wait(sd, m["kv", j])
                    qPE.wait(sd, m["qld", qb])
                gkt = s4 * 4 + kt
                if gkt >= 2:
                    tg = gkt - 2
                    qPE.wait(act, act_pb + tg + 1 +
                             (1 if (tg >= 28 and j >= 1) else 0))
                elif j >= 1:
                    qPE.wait(act, m["exp", j - 1, 30 + gkt])

                def _smm(e, j=j, qb=qb, gkt=gkt):
                    qrv = qr[qb % 2]
                    for hh, base in ((0, 0), (1, 64)):
                        inst = e.matmul(
                            ps_s[gkt % 2][:, hh * 512:(hh + 1) * 512],
                            kp[j % 3][base:base + 64,
                                      gkt * 128:(gkt + 1) * 128],
                            qrv[base:base + 64, :, j % 8, :],
                            start=True, stop=True)
                    return inst
                qPE.run(_smm, inc=pe)
                m["S", j, gkt] = pe.n
                if gkt == 31:
                    m["Sdone", j] = pe.n

            def emit_exp_slice(j, s4, act_pb):
                gi = j * 4 + s4 // 2
                buf = gi % 5
                if s4 % 2 == 0 and gi >= 5:
                    qA.wait(pe, m["od", gi - 5])
                for kt in range(4):
                    gkt = s4 * 4 + kt
                    qA.wait(pe, m["S", j, gkt])
                    lkt = gkt - (s4 // 2) * 8
                    qA.run(lambda e, buf=buf, lkt=lkt, gkt=gkt: e.activation(
                        pt[buf][:, lkt * 1024:(lkt + 1) * 1024],
                        ps_s[gkt % 2][:], AF.Exp, scale=0.125), inc=act)
                    assert act.n == act_pb + gkt + 1 + (
                        1 if (gkt >= 28 and j >= 1) else 0)
                    m["exp", j, gkt] = act.n

            def emit_o_kt(j, s4, kt):
                if s4 == 0 and kt == 0 and j >= 1:
                    qPE.wait(ve, m["omul", j - 1])
                    if j % 8 == 0 and j >= 8:
                        qPE.wait(ve, m["projev", (j // 8 - 1) * 4 + 3])
                gi = j * 4 + s4 // 2
                buf = gi % 5
                gkt = s4 * 4 + kt
                qPE.wait(act, m["exp", j, gkt])
                lkt = gkt - (s4 // 2) * 8

                def _omm(e, j=j, buf=buf, lkt=lkt, gkt=gkt):
                    rhsA = pt[buf][:, lkt * 1024: lkt * 1024 + 512]
                    rhsB = pt[buf][:, lkt * 1024 + 512: (lkt + 1) * 1024]
                    e.matmul(ps_oA[0:64, :],
                             vp[j % 3][:, gkt * 128: gkt * 128 + 64],
                             rhsA, start=(gkt == 0), stop=(gkt == 31))
                    return e.matmul(
                        ps_oB[64:128, :],
                        vp[j % 3][:, gkt * 128 + 64: (gkt + 1) * 128],
                        rhsB, start=(gkt == 0), stop=(gkt == 31),
                        tile_position=(0, 64))
                qPE.run(_omm, inc=pe if kt == 3 else None)
                if kt == 3 and s4 % 2 == 1:
                    m["od", gi] = pe.n
                if kt == 3 and s4 == 7:
                    m["odone", j] = pe.n

            def emit_d_kt(j, s4, kt):
                if s4 == 0 and kt == 0 and j >= 1:
                    qPE.wait(act, m["rev", j - 1])
                gi = j * 4 + s4 // 2
                buf = gi % 5
                gkt = s4 * 4 + kt
                qPE.wait(act, m["exp", j, gkt])
                lkt = gkt - (s4 // 2) * 8

                def _dmm(e, buf=buf, lkt=lkt, gkt=gkt):
                    rhsA = pt[buf][:, lkt * 1024: lkt * 1024 + 512]
                    rhsB = pt[buf][:, lkt * 1024 + 512: (lkt + 1) * 1024]
                    e.matmul(ps_dA[0:1, :], ones_sb[:, 0:1],
                             rhsA, start=(gkt == 0), stop=(gkt == 31))
                    return e.matmul(
                        ps_dB[32:33, :], ones_sb[:, 0:1],
                        rhsB, start=(gkt == 0), stop=(gkt == 31),
                        tile_position=(0, 32))
                qPE.run(_dmm, inc=pe if (kt == 3 and s4 == 7) else None)
                if kt == 3 and s4 == 7:
                    m["ddone", j] = pe.n

            def emit_norm_head(j):
                qV.wait(pe, m["ddone", j])

                def _recip(e):
                    with nc.allow_low_precision(
                            reason="softmax 1/denom feeds bf16 bcast mm"):
                        e.reciprocal(rrow[0:1, :], ps_dA[0:1, :])
                        return e.reciprocal(rrow[32:33, :], ps_dB[32:33, :])
                qV.run(_recip, inc=ve)
                m["recip", j] = ve.n

                qPE.wait(ve, m["recip", j])

                def _bc(e):
                    e.matmul(ps_dA[0:64, :], ones_sb[0:1, 0:64],
                             rrow[0:1, :], start=True, stop=True)
                    return e.matmul(ps_dB[64:128, :], ones_sb[32:33, 0:64],
                                    rrow[32:33, :], start=True, stop=True,
                                    tile_position=(32, 64))
                qPE.run(_bc, inc=pe)
                m["bc", j] = pe.n

                qA.wait(pe, m["bc", j])

                def _rev(e):
                    e.copy(r_sb[0:64, :], ps_dA[0:64, :])
                    return e.copy(r_sb[64:128, :], ps_dB[64:128, :])
                qA.run(_rev, inc=act)
                m["rev", j] = act.n

            def emit_norm_tail(j):
                qV.wait(act, m["rev", j])
                qV.wait(pe, m["odone", j])
                p8 = j % 8
                qb = j // 8
                if qb >= 2 and ("proj", (qb - 2) * 4 + 3) in m:
                    qV.wait(pe, m["proj", (qb - 2) * 4 + 3])

                def _omul(e, p8=p8, qb=qb):
                    oa = o_all[qb % 2]
                    e.tensor_tensor(oa[0:64, p8 * 512:(p8 + 1) * 512],
                                    ps_oA[0:64, :], r_sb[0:64, :], ALU.mult)
                    return e.tensor_tensor(
                        oa[64:128, p8 * 512:(p8 + 1) * 512],
                        ps_oB[64:128, :], r_sb[64:128, :], ALU.mult)
                qV.run(_omul, inc=ve)
                m["omul", j] = ve.n

            def emit_proj(qb):
                for s in range(4):
                    gsub = qb * 4 + s
                    if gsub >= 2:
                        qSY.wait(sd, m["s1w", gsub - 2])
                    qSY.run(lambda e, qb=qb, s=s: e.dma_start(
                        xres[s % 2][:],
                        x_in[qb * 512 + s * 128: qb * 512 + (s + 1) * 128, :]),
                        inc=sd, k=16)
                    m["xres", gsub] = sd.n

                    qPE.wait(ve, m["omul", qb * 8 + 7])
                    if qb == 0 and s == 0:
                        qPE.wait(dw, m["w_proj"])
                    if gsub >= 1:
                        qPE.wait(ve, m["projev", gsub - 1])
                    for nh, pso in ((0, ps_oA), (1, ps_oB)):
                        for pp in range(8):
                            qPE.run(lambda e, qb=qb, s=s, nh=nh, pp=pp, pso=pso:
                                    e.matmul(
                                pso[:],
                                o_all[qb % 2][:, pp * 512 + s * 128: pp * 512 + (s + 1) * 128],
                                w_proj_sb[:, pp * 1024 + nh * 512: pp * 1024 + (nh + 1) * 512],
                                start=(pp == 0), stop=(pp == 7)),
                                inc=pe if (nh == 1 and pp == 7) else None)
                    m["proj", gsub] = pe.n

                    qV.wait(pe, m["proj", gsub])
                    qV.wait(sd, m["xres", gsub])

                    def _prev(e, s=s):
                        e.tensor_tensor(xres[s % 2][:, 0:512], ps_oA[:],
                                        xres[s % 2][:, 0:512], ALU.add)
                        return e.tensor_tensor(xres[s % 2][:, 512:1024], ps_oB[:],
                                               xres[s % 2][:, 512:1024], ALU.add)
                    qV.run(_prev, inc=ve)
                    m["projev", gsub] = ve.n

                    qSY.wait(ve, m["projev", gsub])
                    qSY.run(lambda e, qb=qb, s=s: e.dma_start(
                        s1_scr[qb * 4 + s, :, :], xres[s % 2][:]), inc=sd, k=16)
                    m["s1w", gsub] = sd.n

            emit_qload(0)
            emit_qload(1)
            qSY.run(lambda e: e.dma_start(kp[0][:], k_scr[0, :, :]), inc=sd, k=16)
            qSY.run(lambda e: e.dma_start(vp[0][:], v_scr[0, :, :]), inc=sd, k=16)
            m["kv", 0] = sd.n
            qSY.run(lambda e: e.dma_start(kp[1][:], k_scr[1, :, :]), inc=sd, k=16)
            qSY.run(lambda e: e.dma_start(vp[1][:], v_scr[1, :, :]), inc=sd, k=16)
            m["kv", 1] = sd.n
            qSY.run(lambda e: e.dma_start(kp[2][:], k_scr[2, :, :]), inc=sd, k=16)
            qSY.run(lambda e: e.dma_start(vp[2][:], v_scr[2, :, :]), inc=sd, k=16)
            m["kv", 2] = sd.n

            dlead = {0: [0], 1: [1], 2: [2], 3: [3, 4], 4: [5],
                     5: [6], 6: [7], 7: []}
            for j in range(NJ):
                act_pb = act.n
                for s4 in range(8):
                    for kt in range(4):
                        emit_s_kt(j, s4, kt, act_pb)
                        if j >= 1:
                            for sd4 in dlead[s4]:
                                emit_d_kt(j - 1, sd4, kt)
                            emit_o_kt(j - 1, s4, kt)
                    emit_exp_slice(j, s4, act_pb)
                    if j >= 1 and s4 == 6:
                        emit_norm_head(j - 1)
                    if j >= 1 and s4 == 7:
                        emit_norm_tail(j - 1)
                        emit_prefetch(j)
                        if (j - 1) % 8 == 7:
                            emit_proj((j - 1) // 8)
                            emit_qload((j - 1) // 8 + 2)
            for s4 in range(8):
                for kt in range(4):
                    for sd4 in dlead[s4]:
                        emit_d_kt(NJ - 1, sd4, kt)
                    emit_o_kt(NJ - 1, s4, kt)
            emit_norm_head(NJ - 1)
            emit_norm_tail(NJ - 1)
            emit_proj(NJ // 8 - 1)

            qSY.wait(sd)
            qSY.flush(blk.sync)
            qPE.flush(blk.tensor)
            qV.flush(blk.vector)
            qA.flush(blk.scalar)


    with ExitStack() as _st:
        _sb = lambda n_, s_, d_: _st.enter_context(nc.sbuf_tensor(n_, s_, d_))
        _ps = lambda n_, s_, d_: _st.enter_context(nc.psum_tensor(n_, s_, d_))
        wg = [_sb(f"wg{i}", [128, 8 * 512], BF16) for i in range(2)]
        wu = [_sb(f"wu{i}", [128, 8 * 512], BF16) for i in range(2)]
        wd = [_sb(f"wd{i}", [128, 4 * 1024], BF16) for i in range(2)]
        x1fm = [_sb(f"x1fm{i}", [128, 8 * 1024], BF16) for i in range(2)]
        x1loT = _sb("x1loT", [128, 1024], BF16)
        acc_g = _sb("acc_g", [128, GT * 1024], F32)
        s1t = [_sb(f"s1t{i}", [128, D], F32) for i in range(4)]
        x1t = [_sb(f"x1t{i}", [128, D], F32) for i in range(2)]
        x1v = [_sb(f"x1v{i}", [128, D], F32) for i in range(4)]
        ot = [_sb(f"ot{i}", [128, D], F32) for i in range(2)]
        b_bc = _sb("b_bc", [128, D], F32)
        b_rbf = _sb("b_rbf", [1, D], BF16)
        gate_sb = _sb("gate_sb", [128, 128], F32)
        ghi = _sb("ghi", [128, 128], BF16)
        glo = _sb("glo", [128, 128], BF16)
        gs = [_sb(f"gs{i}", [128, 512], BF16) for i in range(2)]
        guT = [_sb(f"guT{i}", [128, 4 * 512], BF16) for i in range(2)]
        S_all = _sb("S_all", [128, 128], F32)
        Ssig = _sb("Ssig", [128, 128], F32)
        msk_sb = _sb("msk_sb", [128, 16], F32)
        msk2_sb = _sb("msk2_sb", [128, 16], F32)
        sg_sb = _sb("sg_sb", [128, 16], F32)
        ge2_sb = _sb("ge2_sb", [128, 16], F32)
        rt_sb = _sb("rt_sb", [128, 64], F32)
        route_g = [_sb(f"route{i}", [128, GT * 16], F32) for i in range(2)]
        stats = _sb("stats", [128, 128], F32)
        rstats = _sb("rstats", [128, 128], F32)
        eps_sb = _sb("eps_sb", [128, 1], F32)
        eye32_sb = _sb("eye32_sb", [128, 128], F32)
        ps_pre = _ps("ps_pre", [128, 512], F32)
        ps_g = [_ps(f"ps_g{i}", [128, 512], F32) for i in range(2)]
        ps_u = _ps("ps_u", [128, 512], F32)
        ps_dn = [_ps(f"ps_dn{i}", [128, 1024], F32) for i in range(2)]

        def col(k, ti):
            return stats[:, k * 8 + ti: k * 8 + ti + 1]

        def rcol(k, ti):
            return rstats[:, k * 8 + ti: k * 8 + ti + 1]

        def rtc(k, ti):
            return rt_sb[:, k * 8 + ti: k * 8 + ti + 1]

        with nc.Block(no_gpsimd_drain=True) as blk:
            qSY, qPE, qV, qA, qG = Dq(), Dq(), Dq(), Dq(), Dq()
            def _ldb(e):
                return e.dma_start(b_rbf[:], b_proj)
            qG.run(_ldb, inc=dw, k=16)
            m["b_rbf"] = dw.n
            qSY.run(lambda e: e.dma_start(gate_sb[:], gate_pk), inc=sd, k=16)
            qSY.run(lambda e: e.dma_start(eye32_sb[:], eye32_in), inc=sd, k=16)
            m["binit"] = sd.n

            qA.wait(sd, m["binit"])
            qA.run(lambda e: e.copy(ghi[:], gate_sb[:]), inc=act)
            m["bcast0"] = act.n

            qV.wait(act, m["bcast0"])
            qV.run(lambda e: e.memset(eps_sb[:], EPS))
            qV.run(lambda e: e.tensor_tensor(glo[:], gate_sb[:], ghi[:],
                                             ALU.subtract), inc=ve)
            m["glo"] = ve.n

            qPE.wait(dw, m["b_rbf"])
            qPE.wait(ve, m["ones"])
            qPE.run(lambda e: e.matmul(ps_dn[0][:, 0:512], ones_sb[0:1, :],
                                       b_rbf[0:1, 0:512], start=True, stop=True))
            qPE.run(lambda e: e.matmul(ps_dn[0][:, 512:1024], ones_sb[0:1, :],
                                       b_rbf[0:1, 512:1024], start=True,
                                       stop=True), inc=pe)
            m["bps"] = pe.n

            qA.wait(pe, m["bps"])
            qA.run(lambda e: e.copy(b_bc[:], ps_dn[0][:]), inc=act)
            m["bbc"] = act.n
            qSY.flush(blk.sync)
            qPE.flush(blk.tensor)
            qV.flush(blk.vector)
            qA.flush(blk.scalar)
            qG.flush(blk.gpsimd)

        def load_w(qG, g_, e_, buf):
            srcs = ([w_gate[e_], w_up[e_], w_down[e_]] if e_ < NE
                    else [ws_gate, ws_up, ws_down])
            for src, dstt, width in ((srcs[0], wg[buf], 512),
                                     (srcs[1], wu[buf], 512),
                                     (srcs[2], wd[buf], 1024)):
                def _ld(e, src=src, dstt=dstt, width=width):
                    sap = src.rearrange("(c p) m2 -> p c m2", p=128)
                    dap = dstt[:].rearrange("p (c m2) -> p c m2", m2=width)
                    return e.dma_start(dap, sap)
                qG.run(_ld, inc=dw, k=16)

        NGRP = DEV_LIMIT["grp"]
        NEX = DEV_LIMIT["ne"]
        silu_hist, gu_hist, dn_hist, dmm_hist = [], [], [], []
        with nc.Block(no_gpsimd_drain=True) as blk:
            qSY, qPE, qV, qA, qG = Dq(), Dq(), Dq(), Dq(), Dq()

            def emit_pre_tile(gn, ti):
                b4 = ti % 4

                qw = (m.get(("l1ap", gn, ti - 4)) if ti >= 4
                      else m.get(("l1ap", gn - 1, 4 + ti)))
                if qw is not None:
                    qSY.wait(act, qw)
                qSY.run(lambda e, gn=gn, ti=ti, b4=b4: e.dma_start(
                    s1t[b4][:], s1_scr[gn * 8 + ti, :, :]), inc=sd, k=16)
                m["s1ld", gn, ti] = sd.n

                qV.wait(sd, m["s1ld", gn, ti])
                if gn == 0 and ti == 0:
                    qV.wait(act, m["bbc"])
                qV.run(lambda e, b4=b4: e.tensor_tensor(
                    s1t[b4][:], s1t[b4][:], b_bc[:], ALU.add), inc=ve)
                m["v1", gn, ti] = ve.n

                qA.wait(ve, m["v1", gn, ti])
                bi = ti % 2
                qw = (m.get(("tp1", gn, ti - 2)) if ti >= 2
                      else m.get(("tp1", gn - 1, 6 + bi)))
                if qw is not None:
                    qA.wait(pe, qw)
                qw = (m.get(("x1w", gn, ti - 2)) if ti >= 2
                      else m.get(("x1w", gn - 1, 6 + bi)))
                if qw is not None:
                    qA.wait(sd, qw)
                qw = m.get(("l1vn", gn - 1, ti))
                if qw is not None:
                    qA.wait(ve, qw)
                qA.run(lambda e, bi=bi, b4=b4, ti=ti: e.activation(
                    x1t[bi][:], s1t[b4][:], AF.Square, accum_out=col(1, ti)))
                qA.run(lambda e, bi=bi, b4=b4, ti=ti: e.activation(
                    x1t[bi][:], s1t[b4][:], AF.Copy, accum_out=col(0, ti)),
                    inc=act)
                m["l1st", gn, ti] = act.n

                qV.wait(act, m["l1st", gn, ti])
                qw = m.get(("nb1", gn - 1, ti // 4))
                if qw is not None:
                    qV.wait(gp, qw)
                qV.run(lambda e, ti=ti: e.tensor_scalar(
                    col(2, ti), col(0, ti), -INV_D, None, ALU.mult), inc=ve)
                m["l1nm", gn, ti] = ve.n

                qG.wait(ve, m["l1nm", gn, ti])
                qG.run(lambda e, ti=ti: e.tensor_tensor(
                    col(3, ti), col(2, ti), col(2, ti), ALU.mult), inc=gp)
                m["l1pm", gn, ti] = gp.n

                qV.wait(gp, m["l1pm", gn, ti])
                qV.run(lambda e, ti=ti: e.scalar_tensor_tensor(
                    col(4, ti), col(1, ti), INV_D, col(3, ti), ALU.mult,
                    ALU.subtract), inc=ve)
                m["l1vn", gn, ti] = ve.n

                if ti % 4 == 3:
                    b = ti // 4
                    qA.wait(ve, m["l1vn", gn, ti])
                    qA.run(lambda e, b=b: e.activation(
                        rstats[:, 32 + 4 * b: 36 + 4 * b],
                        stats[:, 32 + 4 * b: 36 + 4 * b], AF.Sqrt,
                        bias=eps_sb[:]), inc=act)
                    m["sq1", gn, b] = act.n

                    qV.wait(act, m["sq1", gn, b])
                    def _rc1(e, b=b):
                        with nc.allow_low_precision(
                                reason="f32 reciprocal of LN std"):
                            return e.reciprocal(
                                rstats[:, 40 + 4 * b: 44 + 4 * b],
                                rstats[:, 32 + 4 * b: 36 + 4 * b])
                    qV.run(_rc1, inc=ve)
                    m["rc1", gn, b] = ve.n

                    qG.wait(ve, m["rc1", gn, b])
                    qG.run(lambda e, b=b: e.tensor_tensor(
                        rstats[:, 48 + 4 * b: 52 + 4 * b],
                        stats[:, 16 + 4 * b: 20 + 4 * b],
                        rstats[:, 40 + 4 * b: 44 + 4 * b], ALU.mult), inc=gp)
                    m["nb1", gn, b] = gp.n

            def emit_prefm(gn, tj):
                bj = tj % 2
                b4 = tj % 4
                qA.wait(gp, m["nb1", gn, tj // 4])
                qw = (m.get(("tp1", gn, tj - 2)) if tj >= 2
                      else m.get(("tp1", gn - 1, 6 + bj)))
                if qw is not None:
                    qA.wait(pe, qw)
                qw = (m.get(("x1w", gn, tj - 2)) if tj >= 2
                      else m.get(("x1w", gn - 1, 6 + bj)))
                if qw is not None:
                    qA.wait(sd, qw)
                qA.run(lambda e, bj=bj, b4=b4, tj=tj: e.activation(
                    x1t[bj][:], s1t[b4][:], AF.Identity,
                    scale=rcol(5, tj), bias=rcol(6, tj)), inc=act)
                m["l1ap", gn, tj] = act.n

                qSY.wait(act, m["l1ap", gn, tj])
                qSY.run(lambda e, gn=gn, tj=tj, bj=bj: e.dma_start(
                    x1_scr[gn * 8 + tj, :, :], x1t[bj][:]), inc=sd, k=16)
                m["x1w", gn, tj] = sd.n

                emit_pre_fm(gn, tj)

            def emit_pre_fm(gn, ti):
                bi = ti % 2
                gpar = gn % 2
                prev = (m.get(("scev", gn, ti - 1))
                        if ti >= 1 else m.get(("scev", gn - 1, 7)))
                for p in range(2):
                    qPE.wait(act, m["l1ap", gn, ti])
                    if p == 0:
                        if prev is not None:
                            qPE.wait(act, prev)
                    else:
                        qPE.wait(ve, m["lo", gn, ti, 0])
                    for c2 in range(4):
                        qPE.run(lambda e, bi=bi, p=p, c2=c2: e.transpose(
                            ps_pre[:, c2 * 128:(c2 + 1) * 128],
                            x1t[bi][:, (p * 4 + c2) * 128:
                                    (p * 4 + c2 + 1) * 128],
                            eye32_sb[:]),
                            inc=pe if c2 == 3 else None)
                    m["tp" + str(p), gn, ti] = pe.n

                    qA.wait(pe, m["tp" + str(p), gn, ti])

                    def _hi(e, gpar=gpar, p=p, ti=ti):
                        dst = x1fm[gpar][:].rearrange(
                            "q (c t2) -> q c t2", t2=1024)[
                            :, 4 * p:4 * p + 4, ti * 128:(ti + 1) * 128]
                        src = ps_pre[:].rearrange("q (c t2) -> q c t2", t2=128)
                        return e.copy(dst, src)
                    qA.run(_hi, inc=act)
                    m["hi", gn, ti, p] = act.n

                    qV.wait(act, m["hi", gn, ti, p])
                    w_ = (m.get(("scm", gn, ti - 1)) if ti >= 1
                          else m.get(("scm", gn - 1, 7)))
                    if w_ is not None and p == 0:
                        qV.wait(pe, w_)

                    def _lo(e, gpar=gpar, bi=bi, p=p, ti=ti):
                        dst = x1loT[:].rearrange(
                            "q (c t2) -> q c t2", t2=128)[:, 4 * p:4 * p + 4, :]
                        hi_ = x1fm[gpar][:].rearrange(
                            "q (c t2) -> q c t2", t2=1024)[
                            :, 4 * p:4 * p + 4, ti * 128:(ti + 1) * 128]
                        src = ps_pre[:].rearrange("q (c t2) -> q c t2", t2=128)
                        with nc.allow_low_precision(
                                reason="bf16 lo residual of x1 hi/lo split"):
                            return e.tensor_tensor(dst, src, hi_, ALU.subtract)
                    qV.run(_lo, inc=ve)
                    m["lo", gn, ti, p] = ve.n

                qPE.wait(ve, m["lo", gn, ti, 1])
                if gn == 0 and ti == 0:
                    qPE.wait(ve, m["glo"])
                k = 0
                for xa, ga in ((0, ghi), (1, ghi), (0, glo)):
                    for hc in range(8):
                        def _sc(e, gpar=gpar, bi=bi, ti=ti, xa=xa, ga=ga,
                                hc=hc, k=k):
                            lhs = (x1fm[gpar][:, hc * 1024 + ti * 128:
                                              hc * 1024 + (ti + 1) * 128]
                                   if xa == 0 else
                                   x1loT[:, hc * 128:(hc + 1) * 128])
                            return e.matmul(ps_pre[:, 0:16], lhs,
                                            ga[:, hc * 16:(hc + 1) * 16],
                                            start=(k == 0), stop=(k == 23))
                        qPE.run(_sc, inc=pe if k == 23 else None)
                        k += 1
                m["scm", gn, ti] = pe.n

                qA.wait(pe, m["scm", gn, ti])
                qA.run(lambda e, ti=ti: e.copy(S_all[:, ti * 16:(ti + 1) * 16],
                                               ps_pre[:, 0:16]), inc=act)
                m["scev", gn, ti] = act.n

            def emit_sigb(gn):
                w_ = m.get(("route", gn - 1, 7))
                if w_ is not None:
                    qA.wait(ve, w_)
                w_ = m.get(("rt8", gn - 1, 7))
                if w_ is not None:
                    qA.wait(gp, w_)
                qA.run(lambda e: e.activation(Ssig[:], S_all[:], AF.Sigmoid),
                       inc=act)
                m["sigb", gn] = act.n

            def emit_route_tile(gn, ti):
                z = slice(ti * 16, (ti + 1) * 16)
                qV.wait(act, m["sigb", gn])
                w_ = m.get(("rt3", gn, ti - 1))
                if w_ is not None:
                    qV.wait(gp, w_)
                qV.run(lambda e, ti=ti, z=z: e.reduce_max(
                    rtc(0, ti), Ssig[:, z], axis=AX.X), inc=ve)
                m["rt1", gn, ti] = ve.n

                qG.wait(ve, m["rt1", gn, ti])
                qG.run(lambda e, ti=ti: e.tensor_copy(rtc(5, ti), rtc(0, ti)),
                       inc=gp)
                m["rt1b", gn, ti] = gp.n

                qV.wait(gp, m["rt1b", gn, ti])
                qV.run(lambda e, ti=ti, z=z: e.tensor_scalar(
                    msk_sb[:], Ssig[:, z], rtc(5, ti), -1e9, ALU.is_ge,
                    ALU.mult), inc=ve)
                m["rt2", gn, ti] = ve.n

                qG.wait(ve, m["rt2", gn, ti])
                w_ = m.get(("rt4", gn, ti - 1))
                if w_ is not None:
                    qG.wait(ve, w_)
                qG.run(lambda e, z=z: e.tensor_tensor(
                    msk2_sb[:], msk_sb[:], Ssig[:, z], ALU.add), inc=gp)
                m["rt3", gn, ti] = gp.n

                qV.wait(gp, m["rt3", gn, ti])
                qV.run(lambda e, ti=ti: e.reduce_max(
                    rtc(1, ti), msk2_sb[:], axis=AX.X), inc=ve)
                m["rt4", gn, ti] = ve.n

                qG.wait(ve, m["rt4", gn, ti])
                qG.run(lambda e, ti=ti: e.tensor_copy(rtc(6, ti), rtc(1, ti)),
                       inc=gp)
                m["rt4b", gn, ti] = gp.n

                qA.wait(ve, m["rt4", gn, ti])
                qA.run(lambda e, ti=ti: e.activation(
                    rtc(2, ti), rtc(0, ti), AF.Identity, bias=rtc(1, ti)),
                    inc=act)
                m["rt5", gn, ti] = act.n

                qV.wait(act, m["rt5", gn, ti])
                qV.wait(gp, m["rt4b", gn, ti])
                w_ = m.get(("rt8", gn, ti - 1))
                if w_ is not None:
                    qV.wait(gp, w_)

                def _rt6(e, ti=ti, z=z):
                    e.tensor_scalar(ge2_sb[:], Ssig[:, z], rtc(6, ti), None,
                                    ALU.is_ge)
                    with nc.allow_low_precision(
                            reason="f32 reciprocal of routing denom"):
                        return e.reciprocal(rtc(3, ti), rtc(2, ti))
                qV.run(_rt6, inc=ve)
                m["rt6", gn, ti] = ve.n

                qA.wait(ve, m["rt6", gn, ti])
                qA.run(lambda e, ti=ti: e.activation(
                    rtc(4, ti), rtc(3, ti), AF.Identity, scale=2.5), inc=act)
                m["rt7", gn, ti] = act.n

                qG.wait(ve, m["rt6", gn, ti])
                w_ = m.get(("route", gn, ti - 1))
                if w_ is not None:
                    qG.wait(ve, w_)
                qG.run(lambda e, z=z: e.tensor_tensor(
                    sg_sb[:], Ssig[:, z], ge2_sb[:], ALU.mult), inc=gp)
                m["rt8", gn, ti] = gp.n

                qV.wait(gp, m["rt8", gn, ti])
                qV.wait(act, m["rt7", gn, ti])
                qV.run(lambda e, gn=gn, ti=ti: e.tensor_scalar(
                    route_g[gn % 2][:, ti * 16:(ti + 1) * 16], sg_sb[:],
                    rtc(4, ti), None, ALU.mult), inc=ve)
                m["route", gn, ti] = ve.n

            def emit_v2(g, ti):
                b4 = ti % 4
                qw = (m.get(("l2ap", g, ti - 4)) if ti >= 4
                      else m.get(("l2ap", g - 1, 4 + ti)))
                if qw is not None:
                    qSY.wait(act, qw)
                qSY.run(lambda e, g=g, ti=ti, b4=b4: e.dma_start(
                    x1v[b4][:], x1_scr[g * 8 + ti, :, :]), inc=sd, k=16)
                m["x1vld", g, ti] = sd.n

                qG.wait(sd, m["x1vld", g, ti])
                ge_sh = g * 17 + NE
                qG.wait(ve, m["dn", ge_sh, ti // 4, ti % 4])
                qG.run(lambda e, g=g, ti=ti, b4=b4: e.tensor_tensor(
                    x1v[b4][:], x1v[b4][:],
                    acc_g[:, ti * 1024:(ti + 1) * 1024], ALU.add), inc=gp)
                m["v2", g, ti] = gp.n

            def emit_ln2_stats(g, ti):
                bi2 = ti % 2
                b4 = ti % 4
                qA.wait(gp, m["v2", g, ti])
                qw = (m.get(("outw", g * 8 + ti - 2))
                      or m.get(("outw", (g - 1) * 8 + ti + 6)))
                if qw is not None:
                    qA.wait(sd, qw)
                qw = m.get(("l2vn", g - 1, ti))
                if qw is not None:
                    qA.wait(ve, qw)
                qA.run(lambda e, ti=ti, bi2=bi2, b4=b4: e.activation(
                    ot[bi2][:], x1v[b4][:], AF.Square, accum_out=col(9, ti)))
                qA.run(lambda e, ti=ti, bi2=bi2, b4=b4: e.activation(
                    ot[bi2][:], x1v[b4][:], AF.Copy, accum_out=col(8, ti)),
                    inc=act)
                m["l2st", g, ti] = act.n

                qV.wait(act, m["l2st", g, ti])
                qw = m.get(("nb2", g - 1, ti // 4))
                if qw is not None:
                    qV.wait(gp, qw)
                qV.run(lambda e, ti=ti: e.tensor_scalar(
                    col(10, ti), col(8, ti), -INV_D, None, ALU.mult), inc=ve)
                m["l2nm", g, ti] = ve.n

                qG.wait(ve, m["l2nm", g, ti])
                qG.run(lambda e, ti=ti: e.tensor_tensor(
                    col(11, ti), col(10, ti), col(10, ti), ALU.mult), inc=gp)
                m["l2pm", g, ti] = gp.n

                qV.wait(gp, m["l2pm", g, ti])
                qV.run(lambda e, ti=ti: e.scalar_tensor_tensor(
                    col(12, ti), col(9, ti), INV_D, col(11, ti), ALU.mult,
                    ALU.subtract), inc=ve)
                m["l2vn", g, ti] = ve.n

            def emit_ln2_apply(g, b):
                qA.wait(ve, m["l2vn", g, 4 * b + 3])
                qA.run(lambda e, b=b: e.activation(
                    rstats[:, 96 + 4 * b: 100 + 4 * b],
                    stats[:, 96 + 4 * b: 100 + 4 * b], AF.Sqrt,
                    bias=eps_sb[:]), inc=act)
                m["sq2", g, b] = act.n

                qV.wait(act, m["sq2", g, b])

                def _rc2(e, b=b):
                    with nc.allow_low_precision(
                            reason="f32 reciprocal of LN std"):
                        return e.reciprocal(rstats[:, 104 + 4 * b: 108 + 4 * b],
                                            rstats[:, 96 + 4 * b: 100 + 4 * b])
                qV.run(_rc2, inc=ve)
                m["rc2", g, b] = ve.n

                qG.wait(ve, m["rc2", g, b])
                qG.run(lambda e, b=b: e.tensor_tensor(
                    rstats[:, 112 + 4 * b: 116 + 4 * b],
                    stats[:, 80 + 4 * b: 84 + 4 * b],
                    rstats[:, 104 + 4 * b: 108 + 4 * b], ALU.mult), inc=gp)
                m["nb2", g, b] = gp.n

                for tj in range(4 * b, 4 * b + 4):
                    bj = tj % 2
                    qA.wait(gp, m["nb2", g, b])
                    if tj >= 2:
                        qA.wait(sd, m["outw", g * 8 + tj - 2])
                    qA.run(lambda e, g=g, tj=tj, bj=bj: e.activation(
                        ot[bj][:], x1v[tj % 4][:], AF.Identity,
                        scale=rstats[:, 104 + tj: 105 + tj],
                        bias=rstats[:, 112 + tj: 113 + tj]), inc=act)
                    m["l2ap", g, tj] = act.n

                    qSY.wait(act, m["l2ap", g, tj])
                    qSY.run(lambda e, g=g, tj=tj, bj=bj: e.dma_start(
                        out[(g * 8 + tj) * 128:(g * 8 + tj + 1) * 128, :],
                        ot[bj][:]), inc=sd, k=16)
                    m["outw", g * 8 + tj] = sd.n


            if DEV_LIMIT["pre"]:
                for ti in range(4):
                    emit_pre_tile(0, ti)
                for ti in range(4):
                    emit_prefm(0, ti)
                for ti in range(4, 8):
                    emit_pre_tile(0, ti)
                for ti in range(4, 8):
                    emit_prefm(0, ti)
                emit_sigb(0)
                for ti in range(GT):
                    emit_route_tile(0, ti)

            load_w(qG, 0, 0, 0)
            m["wld", 0] = dw.n

            for g in range(NGRP):
                tasks = []
                if g >= 1:
                    tasks.append(("ln2statsA", g - 1, 0))
                    tasks.append(("ln2mid", g - 1, 0))
                    tasks.append(("ln2statsB", g - 1, 0))
                    tasks.append(("ln2fin", g - 1, 0))
                if g + 1 < NGRP and DEV_LIMIT["pre"]:
                    for ti in range(4):
                        tasks.append(("pre", g + 1, ti))
                    for ti in range(4):
                        tasks.append(("prefm", g + 1, ti))
                    for ti in range(4, 8):
                        tasks.append(("pre", g + 1, ti))
                    for ti in range(4, 8):
                        tasks.append(("prefm", g + 1, ti))
                    tasks.append(("sigbr", g + 1, 0))

                if g >= 1:
                    for ti in range(4):
                        emit_v2(g - 1, ti)

                def run_task(t):
                    kind, a, bx = t
                    if kind == "ln2statsA":
                        for ti in range(4):
                            emit_ln2_stats(a, ti)
                    elif kind == "ln2mid":
                        emit_ln2_apply(a, 0)
                        for ti in range(4, 8):
                            emit_v2(a, ti)
                    elif kind == "ln2statsB":
                        for ti in range(4, 8):
                            emit_ln2_stats(a, ti)
                    elif kind == "ln2fin":
                        emit_ln2_apply(a, 1)
                    elif kind == "pre":
                        emit_pre_tile(a, bx)
                    elif kind == "prefm":
                        emit_prefm(a, bx)
                    elif kind == "sigbr":
                        emit_sigb(a)
                        for ti in range(GT):
                            emit_route_tile(a, ti)

                slot = 0
                for e_ in range(NEX):
                    ge = g * 17 + e_
                    if e_ + 1 < NEX:
                        if ge >= 1:
                            qG.wait(pe, m["edone", ge - 1])
                        load_w(qG, g, e_ + 1, (ge + 1) % 2)
                        m["wld", ge + 1] = dw.n
                    elif g + 1 < NGRP:
                        qG.wait(pe, m["edone", ge - 1])
                        load_w(qG, g + 1, 0, (ge + 1) % 2)
                        m["wld", ge + 1] = dw.n

                    for c in range(2):
                        for mc in range(4):
                            qPE.wait(dw, m["wld", ge])
                            if e_ == 0 and c == 0 and mc == 0:
                                qPE.wait(ve, m["route", g, GT - 1])
                                qPE.wait(act, m["hi", g, 7, 1])
                            if len(silu_hist) >= 2:
                                qPE.wait(act, silu_hist[-2])
                            for hc in range(8):
                                qPE.run(lambda e, ge=ge, g=g, c=c, mc=mc, hc=hc:
                                        e.matmul(
                                            ps_g[mc % 2][:],
                                            wg[ge % 2][:, hc * 512 + mc * 128: hc * 512 + (mc + 1) * 128],
                                            x1fm[g % 2][:, hc * 1024 + c * 512: hc * 1024 + (c + 1) * 512],
                                            start=(hc == 0), stop=(hc == 7)),
                                        inc=pe if hc == 7 else None)
                            m["gmm", ge, c, mc] = pe.n

                            qA.wait(pe, m["gmm", ge, c, mc])
                            if len(gu_hist) >= 2:
                                qA.wait(ve, gu_hist[-2])
                            qA.run(lambda e, mc=mc: e.activation(
                                gs[mc % 2][:], ps_g[mc % 2][:], AF.Silu),
                                inc=act)
                            silu_hist.append(act.n)
                            m["silu", ge, c, mc] = act.n

                            if gu_hist:
                                qPE.wait(ve, gu_hist[-1])
                            for hc in range(8):
                                qPE.run(lambda e, ge=ge, g=g, c=c, mc=mc, hc=hc:
                                        e.matmul(
                                            ps_u[:],
                                            wu[ge % 2][:, hc * 512 + mc * 128: hc * 512 + (mc + 1) * 128],
                                            x1fm[g % 2][:, hc * 1024 + c * 512: hc * 1024 + (c + 1) * 512],
                                            start=(hc == 0), stop=(hc == 7)),
                                        inc=pe if hc == 7 else None)
                            m["umm", ge, c, mc] = pe.n

                            qV.wait(act, m["silu", ge, c, mc])
                            qV.wait(pe, m["umm", ge, c, mc])
                            sec = ge * 2 + c
                            if mc == 0 and ("dnmm", sec - 2) in m:
                                qV.wait(pe, m["dnmm", sec - 2])
                            qV.run(lambda e, mc=mc, sec=sec: e.tensor_tensor(
                                guT[sec % 2][:, mc * 512:(mc + 1) * 512],
                                gs[mc % 2][:], ps_u[:], ALU.mult), inc=ve)
                            gu_hist.append(ve.n)
                            m["gu", ge, c, mc] = ve.n

                        if tasks:
                            run_task(tasks.pop(0))
                        for ti2 in range(4):
                            if len(dn_hist) >= 2:
                                qPE.wait(ve, dn_hist[-2])
                            sec = ge * 2 + c
                            for mc in range(4):
                                if ti2 == 0:
                                    qPE.wait(ve, m["gu", ge, c, mc])
                                for nh in range(2):
                                    qPE.run(lambda e, ge=ge, mc=mc, nh=nh,
                                            ti2=ti2, sec=sec: e.matmul(
                                                ps_dn[ti2 % 2][:, nh * 512:(nh + 1) * 512],
                                                guT[sec % 2][:, mc * 512 + ti2 * 128: mc * 512 + (ti2 + 1) * 128],
                                                wd[ge % 2][:, mc * 1024 + nh * 512: mc * 1024 + (nh + 1) * 512],
                                                start=(mc == 0), stop=(mc == 3)),
                                            inc=pe if (mc == 3 and nh == 1)
                                            else None)
                            dmm_hist.append(pe.n)
                            m["dmm", ge, c, ti2] = pe.n
                            if ti2 == 3:
                                m["dnmm", sec] = pe.n

                            qV.wait(pe, m["dmm", ge, c, ti2])
                            if e_ == 0 and g >= 1:
                                qV.wait(gp, m["v2", g - 1, c * 4 + ti2])

                            def _acc(e, g=g, e_=e_, c=c, ti2=ti2):
                                ti = c * 4 + ti2
                                accs = acc_g[:, ti * 1024:(ti + 1) * 1024]
                                if e_ == 0:
                                    rc = route_g[g % 2][:, ti * 16: ti * 16 + 1]
                                    return e.tensor_scalar(
                                        accs, ps_dn[ti2 % 2][:], rc, None,
                                        ALU.mult)
                                elif e_ < NE:
                                    rc = route_g[g % 2][:, ti * 16 + e_:
                                                        ti * 16 + e_ + 1]
                                    return e.scalar_tensor_tensor(
                                        accs, ps_dn[ti2 % 2][:], rc, accs,
                                        ALU.mult, ALU.add)
                                return e.tensor_tensor(accs, ps_dn[ti2 % 2][:],
                                                       accs, ALU.add)
                            qV.run(_acc, inc=ve)
                            dn_hist.append(ve.n)
                            m["dn", ge, c, ti2] = ve.n
                    m["edone", ge] = pe.n

            if DEV_LIMIT["ln2"]:
                gl = NGRP - 1
                for ti in range(4):
                    emit_v2(gl, ti)
                emit_ln2_stats(gl, 0)
                emit_ln2_stats(gl, 1)
                emit_ln2_stats(gl, 2)
                emit_ln2_stats(gl, 3)
                emit_ln2_apply(gl, 0)
                for ti in range(4, 8):
                    emit_v2(gl, ti)
                for ti in range(4, 8):
                    emit_ln2_stats(gl, ti)
                emit_ln2_apply(gl, 1)

            qSY.wait(sd)
            qSY.flush(blk.sync)
            qPE.flush(blk.tensor)
            qV.flush(blk.vector)
            qA.flush(blk.scalar)
            qG.flush(blk.gpsimd)

    with nc.Block(no_gpsimd_drain=True) as blk:
        @blk.sync
        def _(q):
            if sd.n:
                q.wait_ge(sd.h, sd.n)
            if dw.n:
                q.wait_ge(dw.h, dw.n)
    return nc


def prep_inputs(inputs):
    x = np.ascontiguousarray(np.asarray(inputs["L0_attn_x"], dtype=np.float32))
    w_qkvT = np.ascontiguousarray(
        np.asarray(inputs["L0_attn_w_qkv"], dtype=np.float32).T)
    w_projT = np.ascontiguousarray(
        np.asarray(inputs["L0_attn_w_proj"], dtype=np.float32).T)
    b_proj = np.ascontiguousarray(
        np.asarray(inputs["L0_attn_b_proj"], dtype=np.float32).reshape(1, D))
    gate_w = np.asarray(inputs["L0_ffn_gate_w"], dtype=np.float32)
    gate_pk = np.ascontiguousarray(
        gate_w.T.reshape(8, 128, 16).transpose(1, 0, 2).reshape(128, 128))
    w_gate = np.ascontiguousarray(np.asarray(inputs["L0_ffn_w_gate"], np.float32))
    w_up = np.ascontiguousarray(np.asarray(inputs["L0_ffn_w_up"], np.float32))
    w_down = np.ascontiguousarray(np.asarray(inputs["L0_ffn_w_down"], np.float32))
    ws_gate = np.ascontiguousarray(np.asarray(inputs["L0_ffn_ws_gate"], np.float32))
    ws_up = np.ascontiguousarray(np.asarray(inputs["L0_ffn_ws_up"], np.float32))
    ws_down = np.ascontiguousarray(np.asarray(inputs["L0_ffn_ws_down"], np.float32))

    inv_freq = (1.0 / 10000.0 ** (np.arange(0, HD, 2, dtype=np.float32) / HD)
                ).astype(np.float32)
    t = np.arange(N, dtype=np.float32)
    freqs = (t[:, None] * inv_freq[None, :]).astype(np.float32)
    cos_pk = np.ascontiguousarray(np.cos(freqs).astype(np.float32).reshape(
        NT, 128, 32).transpose(1, 0, 2).reshape(128, NT * 32))
    sin_pk = np.ascontiguousarray(np.sin(freqs).astype(np.float32).reshape(
        NT, 128, 32).transpose(1, 0, 2).reshape(128, NT * 32))
    eye = np.eye(128, dtype=ml_dtypes.bfloat16)

    shared = dict(w_qkvT=w_qkvT, w_projT=w_projT, b_proj=b_proj, gate_pk=gate_pk,
                  w_gate=w_gate, w_up=w_up, w_down=w_down, ws_gate=ws_gate,
                  ws_up=ws_up, ws_down=ws_down, cos_pk=cos_pk, sin_pk=sin_pk,
                  eye_in=eye, eye32_in=np.eye(128, dtype=np.float32))
    return [dict(shared, x=np.ascontiguousarray(x[c])) for c in range(B)]


def kernel(**inputs):
    in_maps = prep_inputs(inputs)
    nc = build_program()
    res = run_bass_kernel_spmd(nc, in_maps, core_ids=list(range(B)))
    return np.stack([np.asarray(res.results[c]["out"]) for c in range(B)], axis=0)
